# version 16
# baseline (speedup 1.0000x reference)
import numpy as np
import ml_dtypes
from contextlib import ExitStack

import concourse.bass as bass
import concourse.mybir as mybir
from concourse.bass_utils import run_bass_kernel_spmd

F32 = mybir.dt.float32
BF16 = mybir.dt.bfloat16
I16 = mybir.dt.int16
I32 = mybir.dt.int32
AF = mybir.ActivationFunctionType
ALU = mybir.AluOpType
AX = mybir.AxisListType
ds = bass.ds

B = 4
D = 1024
S = 4096
NHC = 8
HD = 64
I3 = 3072
TOWN = 2048
NT_OWN = TOWN // 128
NT_ALL = S // 128
DCH = D // 128
ICH = I3 // 128
MCH = 4

RC = float(np.float32(1.5 * 2 ** 23))
INV7 = float(np.float32(1.0 / 7.0))
INV15 = float(np.float32(1.0 / 32767.0))
LN_EPS = 1e-5

ENGS = ("sync", "vector", "scalar", "tensor", "gpsimd")


def enop(eng):
    return eng.drain()


class Chan:
    def __init__(self, nc, ctx, name):
        self.sem = ctx.enter_context(nc.semaphore(name))
        self.n = 0

    def inc(self, instr, k=1):
        instr.then_inc(self.sem, k)
        self.n += k
        return self.n

    def wait(self, eng, n=None):
        eng.wait_ge(self.sem, self.n if n is None else n)


class KB:
    def __init__(self, nc, ctx, block, dbg=()):
        self.nc = nc
        self.ctx = ctx
        self.block = block
        self.dbg = set(dbg)
        self.dbg_t = {}
        ch = lambda name: Chan(nc, ctx, name)
        self.dS = ch("dS")
        self.dG = ch("dG")
        self.cc = ch("cc")
        self.tc = ch("tc")
        self.vc = ch("vc")
        self.sc = ch("sc")
        self.gc = ch("gc")
        self.yc = ch("yc")
        self.chans = {"sync": self.yc, "vector": self.vc, "scalar": self.sc,
                      "tensor": self.tc, "gpsimd": self.gc}
        self.dma_chans = [self.dS, self.dG]
        self.xd = [self.mkdma("xd0"), self.mkdma("xd1")]

    def sb(self, scope, name, shape, dt=F32):
        return scope.enter_context(self.nc.sbuf_tensor(name, shape, dt))

    def mkdma(self, name):
        c = Chan(self.nc, self.ctx, name)
        self.dma_chans.append(c)
        return c

    def borow(self):
        return self.wmr_t[0:1, 0:1024]

    def ps(self, scope, name, shape, dt=F32):
        return scope.enter_context(self.nc.psum_tensor(name, shape, dt))

    def dbg_out(self, name, shape, dt=F32):
        if name in self.dbg:
            t = self.nc.dram_tensor("dbg_" + name, shape, dt,
                                    kind="ExternalOutput").ap()
            self.dbg_t[name] = t
            return t
        return None

    def dbg_dump(self, name, src_ap, shape, dt=F32):
        t = self.dbg_out(name, shape, dt)
        if t is None:
            return
        self.barrier()

        @self.block.sync
        def _(sync):
            self.dS.inc(sync.dma_start(
                out=t[tuple(slice(None) for _ in shape)], in_=src_ap), 16)
            sync.wait_ge(self.dS.sem, self.dS.n)

    def barrier(self):
        marks = {}
        for e in ENGS:
            def body(eng, e=e):
                if e == "sync":
                    for c in self.dma_chans:
                        if c.n:
                            eng.wait_ge(c.sem, c.n)
                if e == "gpsimd":
                    eng.wait_ge(self.cc.sem, self.cc.n)
                marks[e] = self.chans[e].inc(enop(eng))
            getattr(self.block, e)(body)
        for e in ENGS:
            def body(eng, e=e):
                for o in ENGS:
                    if o != e:
                        eng.wait_ge(self.chans[o].sem, marks[o])
            getattr(self.block, e)(body)

    def declare_io(self):
        nc = self.nc
        dr = lambda name, shape, dt=F32: nc.dram_tensor(
            name, shape, dt, kind="ExternalInput").ap()
        self.i_xfull = dr("xfull", [S, D])
        self.i_xmine = dr("xmine", [TOWN, D])
        self.i_wq = dr("wq", [D, 512])
        self.i_wk = dr("wk", [D, 512])
        self.i_wv = dr("wv", [D, 512])
        self.i_bq = dr("bq", [1, 512])
        self.i_bk = dr("bk", [1, 512])
        self.i_bv = dr("bv", [1, 512])
        self.i_wo = dr("wo", [D, D])
        self.i_bo = dr("bo", [1, D])
        self.i_wu = dr("wu", [I3, D])
        self.i_wg = dr("wg", [I3, D])
        self.i_wd = dr("wd", [D, I3])
        self.i_crow = dr("crow", [1, D])
        self.i_wmod = dr("wmod", [D, 6 * D])
        self.i_identb = dr("identb", [128, 128], BF16)
        self.i_identf = dr("identf", [128, 128])
        self.i_maskb = dr("maskb", [128, 128], BF16)
        self.i_onesf = dr("onesf", [1, 128])
        self.i_onesb = dr("onesb", [1, 128], BF16)
        self.i_hselb = dr("hselb", [128, 2], BF16)
        self.i_halfidx = dr("halfidx", [1, 2], I32)
        self.i_eself = dr("eself", [64, 1024])

        self.o_out = nc.dram_tensor("out", [NT_OWN, 128, D], F32,
                                    kind="ExternalOutput").ap()

        dri = lambda name, shape, dt=F32: nc.dram_tensor(name, shape, dt,
                                                         kind="Internal")
        self.h_gate = dri("gate_raw_h", [ICH, 128, TOWN]).ap()
        self.h_up = dri("up_raw_h", [ICH, 128, TOWN]).ap()
        self.h_o = dri("o_raw_h", [ICH, 128, TOWN]).ap()
        self.h_wu = dri("wu_deq_h", [DCH, ICH, 128, 128], BF16).ap()
        self.h_wg = dri("wg_deq_h", [DCH, ICH, 128, 128], BF16).ap()
        self.h_wd = dri("wd_deq_h", [ICH, DCH, 128, 128], BF16).ap()
        self.h_xnew = dri("xnew_h", [NT_OWN, 128, D]).ap()
        self.g_ag1_in = dri("ag1_in", [TOWN, 512], BF16)
        self.g_ag2_in = dri("ag2_in", [TOWN, 512], BF16)
        self.g_ag_out = dri("ag_out", [4 * TOWN, 512], BF16)
        self.g_ar_in = [dri(f"ar_in{i}", [1, 256]) for i in range(4)]
        self.h_mrow = dri("mrow_h", [48, 128]).ap()
        self.g_ar_out = [dri(f"ar_out{i}", [1, 256]) for i in range(4)]

    def alloc_perm(self, sc):
        self.identb = self.sb(sc, "identb_s", [128, 128], BF16)
        self.identf = self.sb(sc, "identf_s", [128, 128])
        self.maskb = self.sb(sc, "maskb_s", [128, 128], BF16)
        self.onesf = self.sb(sc, "onesf_s", [1, 128])
        self.onesb = self.sb(sc, "onesb_s", [1, 128], BF16)
        self.hselb = self.sb(sc, "hselb_s", [128, 2], BF16)
        self.mod_fm = self.sb(sc, "mod_fm", [128, 128])
        self.modb = self.sb(sc, "modb", [128, 4, D])
        self.gabo = self.sb(sc, "gabo", [128, D])
        self.bqkv = self.sb(sc, "bqkv", [128, 3, MCH])
        self.bvb = self.sb(sc, "bvb", [128, 512])
        self.scales = self.sb(sc, "scales", [128, 16])
        self.expb = self.sb(sc, "expb", [128, 4])
        self.small = self.sb(sc, "small", [128, 64])
        self.row = self.sb(sc, "row", [1, 1024])
        self.silub = self.sb(sc, "silub", [128, 8], BF16)
        self.hofs = self.sb(sc, "hofs", [1, 2], I32)

    def p0(self, sc):
        nc, block = self.nc, self.block
        wmr = self.sb(sc, "wmr", [128, 3 * D // 2])
        self.wmr_t = wmr
        psA, psB = self.psA, self.psB
        psF = self.psC[:, 0:128]

        @block.sync
        def _(sync):
            for dst, src in ((self.identb, self.i_identb),
                             (self.identf, self.i_identf),
                             (self.maskb, self.i_maskb),
                             (self.onesf, self.i_onesf),
                             (self.onesb, self.i_onesb),
                             (self.hselb, self.i_hselb),
                             (self.hofs, self.i_halfidx)):
                self.dS.inc(sync.dma_start(out=dst[:, :], in_=src[:, :]), 16)
            self.dS.inc(sync.dma_start(
                out=self.small[:, 0:8],
                in_=self.i_crow.rearrange("o (c p) -> p (o c)", p=128),
                allow_slow_non_contiguous=True), 16)
            self.dS.inc(sync.dma_start(
                out=self.small[:, 8:12],
                in_=self.i_bq.rearrange("o (c p) -> p (o c)", p=128),
                allow_slow_non_contiguous=True), 16)
            self.dS.inc(sync.dma_start(
                out=self.small[:, 12:16],
                in_=self.i_bk.rearrange("o (c p) -> p (o c)", p=128),
                allow_slow_non_contiguous=True), 16)
            self.dS.inc(sync.dma_start(
                out=self.small[:, 16:20],
                in_=self.i_bv.rearrange("o (c p) -> p (o c)", p=128),
                allow_slow_non_contiguous=True), 16)
            self.dS.inc(sync.dma_start(out=self.row[0:1, 0:512],
                                       in_=self.i_bv[0:1, :]), 16)


        @block.gpsimd
        def _(g):
            for src, dst in ((self.i_wq, self.wq_s), (self.i_wk, self.wk_s),
                             (self.i_wv, self.wv_s)):
                sr = src.rearrange("(c p) m -> c p m", p=128)
                for c in range(DCH):
                    self.dG.inc(g.dma_start(out=dst[:, c, :], in_=sr[c, :, :]), 16)
            g.memset(self.v_tm[:, :, :].rearrange(
                "p t (h e) -> p t h e", e=65)[:, :, :, 64:65], 1.0)
            self.gc.inc(enop(g))
        self.barrier()

        @block.vector
        def _(v):
            v.memset(self.scales[:, :], 0.0)

        @block.scalar
        def _(s):
            s.activation(self.small[:, 32:40], self.small[:, 0:8], AF.Silu)
            self.sc.inc(enop(s))
        s_silu = self.sc.n

        @block.vector
        def _(v):
            self.sc.wait(v, s_silu)
            self.vc.inc(v.tensor_copy(self.silub[:, 0:8], self.small[:, 32:40]))
        v_silub = self.vc.n
        silu_f = self.small[:, 32:40]

        t0 = self.tc.n
        dg0 = self.dG.n

        @block.gpsimd
        def _(g):
            for ch in range(4 * DCH):
                c, qt = ch // 4, ch % 4
                if ch > 0:
                    g.wait_ge(self.tc.sem, t0 + ch)
                self.dG.inc(g.dma_start(
                    out=wmr[:, :],
                    in_=self.i_wmod[c * 128:(c + 1) * 128,
                                    qt * 1536:(qt + 1) * 1536]), 16)

        @block.tensor
        def _(t):
            self.vc.wait(t, v_silub)
            for ch in range(4 * DCH):
                c, qt = ch // 4, ch % 4
                t.wait_ge(self.dG.sem, dg0 + 16 * (ch + 1))
                for gg in range(12):
                    gcol = qt * 12 + gg
                    t.matmul(psA[:, gcol * 8 + c:gcol * 8 + c + 1],
                             wmr[:, gg * 128:(gg + 1) * 128],
                             silu_f[:, c:c + 1],
                             start=True, stop=True)
                self.tc.inc(enop(t))
        self.barrier()

        @block.vector
        def _(v):
            v.memset(self.mod_fm[:, 48:128], 0.0)
            v.tensor_reduce(self.mod_fm[:, 0:48],
                            psA[:, 0:384].rearrange("p (g c) -> p g c", c=8),
                            AX.X, ALU.add)
            v.drain()
            v.tensor_scalar_add(self.mod_fm[:, 8:16], self.mod_fm[:, 8:16], 1.0)
            v.tensor_scalar_add(self.mod_fm[:, 32:40], self.mod_fm[:, 32:40], 1.0)
            for lo in (16, 40):
                g = self.mod_fm[:, lo:lo + 8]
                v.drain()
                v.tensor_scalar(self.small[:, 40:48], g, -1.0, None, ALU.mult)
                v.drain()
                v.tensor_tensor(self.small[:, 40:48], g, self.small[:, 40:48],
                                ALU.max)
                v.drain()
                v.tensor_scalar_max(self.small[:, 40:48], self.small[:, 40:48],
                                    0.25)
                v.tensor_scalar(self.small[:, 48:56], g, 0.0, 2.0,
                                ALU.is_ge, ALU.mult)
                v.drain()
                v.tensor_scalar_add(self.small[:, 48:56], self.small[:, 48:56],
                                    -1.0)
                v.drain()
                v.tensor_tensor(g, self.small[:, 40:48], self.small[:, 48:56],
                                ALU.mult)
            v.drain()
            v.tensor_copy(self.silub[:, 0:8], self.mod_fm[:, 0:8])
        self.barrier()

        @block.tensor
        def _(t):
            for pj, w in enumerate((self.wq_s, self.wk_s, self.wv_s)):
                for mc in range(MCH):
                    j = pj * MCH + mc
                    for c in range(DCH):
                        t.matmul(psB[:, j:j + 1],
                                 w[:, c, mc * 128:(mc + 1) * 128],
                                 self.silub[:, c:c + 1],
                                 start=(c == 0), stop=(c == DCH - 1))
            for c in range(DCH):
                t.matmul(psA[0:1, 0:512], self.silub[:, c:c + 1],
                         self.wv_s[:, c, :],
                         start=(c == 0), stop=(c == DCH - 1))
            t.transpose(psF, self.mod_fm[:, :], self.identf[:, :])
        self.barrier()

        @block.vector
        def _(v):
            for pj in range(3):
                v.tensor_tensor(self.bqkv[:, pj, :],
                                self.small[:, 8 + 4 * pj:12 + 4 * pj],
                                psB[:, pj * MCH:(pj + 1) * MCH], ALU.add)
            v.tensor_tensor(self.row[0:1, 0:512], self.row[0:1, 0:512],
                            psA[0:1, 0:512], ALU.add)
        self.barrier()

        @block.vector
        def _(v):
            for w in (self.wq_s, self.wk_s, self.wv_s):
                for c in range(DCH):
                    i = v.tensor_scalar_mul(w[:, c, :], w[:, c, :],
                                            self.mod_fm[:, 8 + c:9 + c])
            self.vc.inc(i)
        v_wsc = self.vc.n

        @block.vector
        def _(v):
            v.tensor_copy(self.scr128[:, :], psF)
        self.barrier()

        @block.gpsimd
        def _(g):
            self.dG.inc(g.dma_start(out=self.h_mrow[:, :],
                                    in_=self.scr128[0:48, :]), 16)
        self.barrier()
        for gi, lo in enumerate((32, 24, 16, 40)):
            @block.gpsimd
            def _(g, gi=gi, lo=lo):
                self.dG.inc(g.dma_start(
                    out=self.rows4[0:1, :, :],
                    in_=self.h_mrow[lo:lo + 8, :].rearrange(
                        "a b -> (a b)").unsqueeze(0)), 16)
            self.barrier()
            t1 = self.tc.n
            v1 = self.vc.n

            @block.tensor
            def _(t, gi=gi):
                for k in range(8):
                    if k >= 2:
                        t.wait_ge(self.vc.sem, v1 + k - 1)
                    t.matmul(self.psD[:, 0:128] if k % 2 == 0
                             else self.psD[:, 128:256],
                             self.onesf[0:1, :], self.rows4[0:1, k, :],
                             start=True, stop=True)
                    self.tc.inc(enop(t))

            @block.vector
            def _(v, gi=gi):
                for k in range(8):
                    v.wait_ge(self.tc.sem, t1 + k + 1)
                    src = (self.psD[:, 0:128] if k % 2 == 0
                           else self.psD[:, 128:256])
                    self.vc.inc(v.tensor_copy(
                        self.modb[:, gi, k * 128:(k + 1) * 128], src))
            self.barrier()

        @block.sync
        def _(sync):
            self.dS.inc(sync.dma_start(out=self.borow()[0:1, 0:D],
                                       in_=self.i_bo[0:1, :]), 16)
        self.barrier()

        @block.tensor
        def _(t):
            t.matmul(psA[:, 0:512], self.onesf[0:1, :], self.row[0:1, 0:512],
                     start=True, stop=True)
            t.matmul(psB[:, 0:512], self.onesf[0:1, :],
                     self.borow()[0:1, 0:512], start=True, stop=True)
            t.matmul(self.psD[:, 0:512], self.onesf[0:1, :],
                     self.borow()[0:1, 512:1024], start=True, stop=True)
        self.barrier()

        @block.vector
        def _(v):
            v.memset(self.small[:, 63:64], LN_EPS)
            v.tensor_copy(self.bvb[:, :], psA[:, 0:512])
            v.tensor_copy(self.gabo[:, 0:512], psB[:, 0:512])
            v.tensor_copy(self.gabo[:, 512:1024], self.psD[:, 0:512])
            v.drain()
            v.tensor_tensor(self.gabo[:, :], self.gabo[:, :],
                            self.modb[:, 2, :], ALU.mult)
            v.tensor_copy(self.row[0:1, 200:202], self.hofs[0:1, :])
        self.barrier()

        @block.tensor
        def _(t):
            t.matmul(self.psD[:, 0:1], self.onesf[0:1, :],
                     self.row[0:1, 200:201], start=True, stop=True)
        self.barrier()

        @block.vector
        def _(v):
            v.tensor_copy(self.scales[:, 14:15], self.psD[:, 0:1])
        self.barrier()

    def p1(self):
        nc, block = self.nc, self.block
        hfm, xst, ht, psT = self.hfm, self.xst, self.ht, self.psT

        V0, T0, S0, D0 = self.vc.n, self.tc.n, self.sc.n, self.dS.n

        def v_stats(j):   return V0 + 32 * (j // 4) + 2 * (j % 4) + 1
        def v_apply(j):   return V0 + 32 * (j // 4) + 2 * (j % 4) + 2
        def v_hfm(j):     return V0 + 32 * (j // 4) + 8 + (j % 4) + 1
        def v_qkv(w, g):  return V0 + 32 * w + 12 + g + 1
        def v_nrm(w, i):  return V0 + 32 * w + 24 + i + 1
        def t_tr(j):      return T0 + 24 * (j // 4) + (j % 4) + 1
        def t_qkv(w, g):  return T0 + 24 * w + 4 + g + 1
        def t_nrm(w, i):  return T0 + 24 * w + 16 + i + 1
        def s_sqrt(j):    return S0 + 12 * (j // 4) + (j % 4) + 1
        def s_sq(w, i):   return S0 + 12 * w + 4 + i + 1
        @block.sync
        def _(sync):
            xf = self.i_xfull.rearrange("(t p) d -> t p d", p=128)
            for j in range(NT_ALL):
                if j >= 2:
                    sync.wait_ge(self.vc.sem, v_apply(j - 2))
                self.xd[j % 2].inc(sync.dma_start(out=xst[:, j % 2, :],
                                                  in_=xf[j, :, :]), 16)

        @block.vector
        def _(v):
            for w in range(8):
                for j4 in range(4):
                    j = 4 * w + j4
                    xt = xst[:, j % 2, :]
                    v.wait_ge(self.xd[j % 2].sem, 16 * (j // 2 + 1))
                    v.bn_stats(self.bn6[:, 0:6], xt[:, 0:512])
                    v.bn_stats(self.bn6[:, 6:12], xt[:, 512:1024])
                    v.drain()
                    v.bn_aggr(self.stats[:, 2 * j4:2 * j4 + 2], self.bn6[:, :])
                    self.vc.inc(enop(v))
                    assert self.vc.n == v_stats(j)
                    v.wait_ge(self.sc.sem, s_sqrt(j))
                    if j >= 2:
                        v.wait_ge(self.tc.sem, t_tr(j - 2))
                    v.reciprocal(self.stats[:, 8 + j4:9 + j4],
                                 self.stats[:, 12 + j4:13 + j4])
                    v.drain()
                    self.vc.inc(v.tensor_scalar(
                        ht[:, j % 2, :], xt,
                        self.stats[:, 2 * j4:2 * j4 + 1],
                        self.stats[:, 8 + j4:9 + j4],
                        ALU.subtract, ALU.mult))
                    assert self.vc.n == v_apply(j)
                for j4 in range(4):
                    j = 4 * w + j4
                    v.wait_ge(self.tc.sem, t_tr(j))
                    i = v.tensor_copy(
                        hfm[:, w % 2, :, j4 * 128:(j4 + 1) * 128],
                        psT[:, j % 2, :].rearrange("p (c t) -> p c t", c=DCH))
                    self.vc.inc(i)
                    assert self.vc.n == v_hfm(j)
                for g in range(12):
                    v.wait_ge(self.tc.sem, t_qkv(w, g))
                    pq = self.qkv_ps[g % 4]
                    if g < 4:
                        i = v.tensor_scalar_add(
                            self.q_fm[:, g, w * 512:(w + 1) * 512],
                            pq[:, :], self.bqkv[:, 0, g:g + 1])
                    elif g < 8:
                        i = v.tensor_scalar_add(
                            self.k_fm[:, g - 4, w * 512:(w + 1) * 512],
                            pq[:, :], self.bqkv[:, 1, g - 4:g - 3])
                    else:
                        j = 4 * w + (g - 8)
                        i = v.tensor_tensor(
                            self.v_tm[:, j, :].rearrange(
                                "p (h e) -> p h e", h=8)[:, :, 0:64],
                            pq[:, :].rearrange("p (h e) -> p h e", h=8),
                            self.bvb[:, :].rearrange("p (h e) -> p h e", h=8),
                            ALU.add)
                    self.vc.inc(i)
                    assert self.vc.n == v_qkv(w, g)
                for i in range(8):
                    v.wait_ge(self.tc.sem, t_nrm(w, i))
                    ii = v.tensor_reduce(self.qnst[0:2, i, w:w + 1],
                                         self.psN[0:2, :], AX.X, ALU.max)
                    self.vc.inc(ii)
                    assert self.vc.n == v_nrm(w, i)

        @block.scalar
        def _(s):
            for w in range(8):
                for j4 in range(4):
                    j = 4 * w + j4
                    s.wait_ge(self.vc.sem, v_stats(j))
                    s.activation(self.stats[:, 12 + j4:13 + j4],
                                 self.stats[:, 2 * j4 + 1:2 * j4 + 2], AF.Sqrt,
                                 bias=self.small[:, 63:64], scale=1.0)
                    self.sc.inc(enop(s))
                    assert self.sc.n == s_sqrt(j)
                for i in range(8):
                    qk, mc = i // 4, i % 4
                    s.wait_ge(self.vc.sem, v_qkv(w, 4 * qk + mc))
                    if not (w == 0 and i == 0):
                        pw, pi = (w, i - 1) if i > 0 else (w - 1, 7)
                        s.wait_ge(self.tc.sem, t_nrm(pw, pi))
                    src = self.q_fm if qk == 0 else self.k_fm
                    self.sc.inc(s.activation(
                        self.sqb[:, :],
                        src[:, mc, w * 512:(w + 1) * 512], AF.Square))
                    assert self.sc.n == s_sq(w, i)

        @block.tensor
        def _(t):
            for w in range(8):
                for j4 in range(4):
                    j = 4 * w + j4
                    t.wait_ge(self.vc.sem, v_apply(j))
                    if j >= 2:
                        t.wait_ge(self.vc.sem, v_hfm(j - 2))
                    for c in range(DCH):
                        i = t.transpose(psT[:, j % 2, c * 128:(c + 1) * 128],
                                        ht[:, j % 2, c * 128:(c + 1) * 128],
                                        self.identb[:, :])
                    self.tc.inc(i)
                    assert self.tc.n == t_tr(j)
                t.wait_ge(self.vc.sem, v_hfm(4 * w + 3))
                for g in range(12):
                    if w > 0 or g >= 4:
                        pw, pg = (w, g - 4) if g >= 4 else (w - 1, g + 8)
                        t.wait_ge(self.vc.sem, v_qkv(pw, pg))
                    pq = self.qkv_ps[g % 4]
                    if g < 8:
                        wgt = self.wq_s if g < 4 else self.wk_s
                        mc = g % 4
                        for c in range(DCH):
                            i = t.matmul(pq[:, :],
                                         wgt[:, c, mc * 128:(mc + 1) * 128],
                                         hfm[:, w % 2, c, :],
                                         start=(c == 0), stop=(c == DCH - 1))
                    else:
                        j4 = g - 8
                        for c in range(DCH):
                            i = t.matmul(pq[:, :],
                                         hfm[:, w % 2, c,
                                             j4 * 128:(j4 + 1) * 128],
                                         self.wv_s[:, c, :],
                                         start=(c == 0), stop=(c == DCH - 1))
                    self.tc.inc(i)
                    assert self.tc.n == t_qkv(w, g)
                for i in range(8):
                    t.wait_ge(self.sc.sem, s_sq(w, i))
                    if not (w == 0 and i == 0):
                        pw, pi = (w, i - 1) if i > 0 else (w - 1, 7)
                        t.wait_ge(self.vc.sem, v_nrm(pw, pi))
                    self.tc.inc(t.matmul(self.psN[:, :],
                                         self.hselb[:, :], self.sqb[:, :],
                                         start=True, stop=True))
                    assert self.tc.n == t_nrm(w, i)

        self.barrier()

    def p1b(self):
        block = self.block

        @block.vector
        def _(v):
            v.tensor_reduce(self.small[0:2, 0:8], self.qnst[:, :, :],
                            AX.X, ALU.max)
            v.drain()
            v.tensor_tensor(self.small[0:2, 8:12], self.small[0:2, 0:4],
                            self.small[0:2, 4:8], ALU.mult)
            self.vc.inc(enop(v))
        v_b2 = self.vc.n

        @block.gpsimd
        def _(g):
            g.wait_ge(self.vc.sem, v_b2)
            self.dG.inc(g.dma_start(
                out=self.row[0:1, 0:8:2], in_=self.small[0:1, 8:12],
                allow_slow_non_contiguous=True), 16)
            self.dG.inc(g.dma_start(
                out=self.row[0:1, 1:8:2], in_=self.small[1:2, 8:12],
                allow_slow_non_contiguous=True), 16)
        self.barrier()

        @block.scalar
        def _(s):
            s.activation(self.row[0:1, 8:16], self.row[0:1, 0:8], AF.Sqrt)
        self.barrier()

        @block.vector
        def _(v):
            v.tensor_tensor(self.row[0:1, 16:20], self.row[0:1, 8:16:2],
                            self.row[0:1, 9:16:2], ALU.max)
            v.drain()
            v.tensor_scalar_mul(self.row[0:1, 16:20], self.row[0:1, 16:20],
                                -0.125)
            v.drain()
            v.tensor_copy(self.silub[0:1, 0:4], self.row[0:1, 16:20])
        self.barrier()

        @block.tensor
        def _(t):
            t.matmul(self.psD[:, 0:4], self.onesb[0:1, :], self.silub[0:1, 0:4],
                     start=True, stop=True)
        self.barrier()

        @block.vector
        def _(v):
            v.tensor_copy(self.expb[:, :], self.psD[:, 0:4])
        self.barrier()


    def p2(self, NP, NCH):
        block = self.block
        S2 = (self.S2a, self.S2b)
        PVp = self.PVp
        ptx = (self.ptx0, self.ptx1)
        wld, bm, deqc, wTst = self.wld, self.bm, self.deqc, self.wTst

        groups = [(q4, hp, e) for q4 in range(4) for hp in range(4)
                  for e in range(2)]
        NG = len(groups)
        ngs = [8 * (q4 + 1) for (q4, hp, e) in groups]

        def pieces_of(off):
            return [(off, 512), (512, 1024)] if off < 512 else [(off, 1024)]

        def src_ap(k):
            if k < 24:
                return self.i_wu[k * 128:(k + 1) * 128, :]
            if k < 48:
                return self.i_wg[(k - 24) * 128:(k - 23) * 128, :]
            kk = k - 48
            r, q3 = kk // 3, kk % 3
            return self.i_wd[r * 128:(r + 1) * 128, q3 * 1024:(q3 + 1) * 1024]

        def dst_ap(k):
            if k < 24:
                return self.h_wu[:, k, :, :].rearrange("d p j -> p d j")
            if k < 48:
                return self.h_wg[:, k - 24, :, :].rearrange("d p j -> p d j")
            kk = k - 48
            r, q3 = kk // 3, kk % 3
            return self.h_wd[q3 * 8:(q3 + 1) * 8, r, :, :].rearrange(
                "i p j -> p i j")

        wl = [self.mkdma(f"wlb{i}") for i in range(4)]
        wt = [self.mkdma(f"wtb{i}") for i in range(2)]

        @block.vector
        def _(v):
            v.memset(S2[0][:, :], 0.0)
            v.memset(S2[1][:, :], 0.0)
            v.memset(PVp[:, :], 0.0)
        self.barrier()
        V0b, T0b, S0b = self.vc.n, self.tc.n, self.sc.n

        TSC, TPV, TTB, TTR = {}, {}, {}, {}
        SEX, VEV, VNM, VDQ, VET = {}, {}, {}, {}, {}
        tn = self.tc.n
        for g in range(NG):
            n_g = ngs[g]
            for c in range(n_g):
                tn += 1; TSC[(g, c)] = tn
                if c >= 1:
                    tn += 1; TPV[(g, c - 1)] = tn
            tn += 1; TPV[(g, n_g - 1)] = tn
            tn += 1; TTB[g] = tn
            p = g - 1
            if 0 <= p < NP:
                tn += 1; TTR[2 * p] = tn
                tn += 1; TTR[2 * p + 1] = tn
        for p in range(NG - 1, NP):
            tn += 1; TTR[2 * p] = tn
            tn += 1; TTR[2 * p + 1] = tn
        sn = self.sc.n
        for g in range(NG):
            for c in range(ngs[g]):
                sn += 1; SEX[(g, c)] = sn
        vn = self.vc.n
        for g in range(NG):
            vn += 1; VEV[g] = vn
            vn += 1; VNM[g] = vn
            p = g - 1
            if 0 <= p < NP:
                vn += 1; VET[2 * p] = vn
                vn += 1; VET[2 * p + 1] = vn
            if g < NP:
                vn += 1; VDQ[g] = vn
        for p in range(NG, NP + NG - 1):
            if p < NP:
                vn += 1; VDQ[p] = vn
            pe = p - 1
            if NG - 1 <= pe < NP:
                vn += 1; VET[2 * pe] = vn
                vn += 1; VET[2 * pe + 1] = vn

        def emit_deq_pair(v, p):
            ka, kb = 2 * p, 2 * p + 1
            v.wait_ge(wl[ka % 4].sem, 16 * (ka // 4 + 1))
            v.wait_ge(wl[kb % 4].sem, 16 * (kb // 4 + 1))
            wa, wb = wld[:, ka % 4, :], wld[:, kb % 4, :]
            wa3 = wa.rearrange("p (b j) -> p b j", j=32)
            wb3 = wb.rearrange("p (b j) -> p b j", j=32)
            ba, bb = bm[:, 0, :], bm[:, 1, :]
            v.tensor_reduce(ba[0:128, 0:32], wa3, AX.X, ALU.max,
                            apply_absolute_value=True)
            v.tensor_reduce(bb[0:128, 0:32], wb3, AX.X, ALU.max,
                            apply_absolute_value=True)
            v.drain()
            v.tensor_scalar(ba[0:128, 32:64], ba[0:128, 0:32], INV7, 1e-12,
                            ALU.mult, ALU.max)
            v.tensor_scalar(bb[0:128, 32:64], bb[0:128, 0:32], INV7, 1e-12,
                            ALU.mult, ALU.max)
            v.drain()
            v.reciprocal(ba[0:128, 64:96], ba[0:128, 32:64])
            v.reciprocal(bb[0:128, 64:96], bb[0:128, 32:64])
            v.drain()
            v.tensor_tensor(wa3, wa3, ba[0:128, 64:96].unsqueeze(2)
                            .broadcast_to((128, 32, 32)), ALU.mult)
            v.tensor_tensor(wb3, wb3, bb[0:128, 64:96].unsqueeze(2)
                            .broadcast_to((128, 32, 32)), ALU.mult)
            v.drain()
            v.tensor_scalar(wa, wa, RC, RC, ALU.add, ALU.subtract)
            v.tensor_scalar(wb, wb, RC, RC, ALU.add, ALU.subtract)
            v.drain()
            if p >= 1:
                v.wait_ge(self.tc.sem, TTR[2 * p - 1])
            v.tensor_tensor(deqc[:, 0, :].rearrange("p (b j) -> p b j", j=32),
                            wa3, ba[0:128, 32:64].unsqueeze(2)
                            .broadcast_to((128, 32, 32)), ALU.mult)
            self.vc.inc(v.tensor_tensor(
                deqc[:, 1, :].rearrange("p (b j) -> p b j", j=32),
                wb3, bb[0:128, 32:64].unsqueeze(2)
                .broadcast_to((128, 32, 32)), ALU.mult))
            assert self.vc.n == VDQ[p]

        def emit_evac_pair(v, p):
            for k in (2 * p, 2 * p + 1):
                v.wait_ge(self.tc.sem, TTR[k])
                if k >= 2:
                    v.wait_ge(wt[k % 2].sem, 16 * (k // 2))
                self.vc.inc(v.tensor_copy(
                    wTst[:, k % 2, :, :],
                    ptx[k % 2][:, :].rearrange("p (c j) -> p c j", j=128)))
                assert self.vc.n == VET[k]

        def emit_trans_pair(t, p):
            t.wait_ge(self.vc.sem, VDQ[p])
            if p >= 1:
                t.wait_ge(self.vc.sem, VET[2 * p - 1])
            for k in (2 * p, 2 * p + 1):
                for kb8 in range(8):
                    i = t.transpose(ptx[k % 2][:, kb8 * 128:(kb8 + 1) * 128],
                                    deqc[:, k % 2, kb8 * 128:(kb8 + 1) * 128],
                                    self.identb[:, :])
                self.tc.inc(i)
                assert self.tc.n == TTR[k]

        @block.tensor
        def _(t):
            for g, (q4, hp, e) in enumerate(groups):
                n_g = ngs[g]
                qb = 1024 * q4
                npp = ngs[g - 1] if g else 0
                for c in range(n_g):
                    if c >= 2:
                        t.wait_ge(self.sc.sem, SEX[(g, c - 2)])
                    elif g >= 1:
                        if c == 0:
                            t.wait_ge(self.sc.sem, SEX[(g - 1, npp - 2)])
                        else:
                            t.wait_ge(self.sc.sem, SEX[(g - 1, npp - 1)])
                            t.wait_ge(self.vc.sem, VNM[g - 1])
                    off = max(0, c * 128 - qb)
                    diag = c * 128 >= qb
                    for (a, b) in pieces_of(off):
                        i = t.matmul(
                            S2[c % 2][:, a:b],
                            self.k_fm[64 * e:64 * e + 64, hp,
                                      c * 128:(c + 1) * 128],
                            self.q_fm[64 * e:64 * e + 64, hp, qb + a:qb + b],
                            start=True, stop=not (diag and a <= off < b),
                            tile_position=(64 * e, 0))
                    if diag:
                        i = t.matmul(S2[c % 2][:, off:off + 128],
                                     self.identb[:, :], self.maskb[:, :],
                                     start=False, stop=True)
                    self.tc.inc(i)
                    assert self.tc.n == TSC[(g, c)]
                    if c >= 1:
                        cp = c - 1
                        t.wait_ge(self.sc.sem, SEX[(g, cp)])
                        if cp == 0 and g >= 1:
                            t.wait_ge(self.vc.sem, VEV[g - 1])
                        offp = max(0, cp * 128 - qb)
                        he = 2 * hp + e
                        for (a, b) in pieces_of(offp):
                            i = t.matmul(
                                PVp[0:65, a:b],
                                self.v_tm[:, cp, he * 65:he * 65 + 65],
                                self.p_buf[:, cp % 2,
                                           1024 * e + a:1024 * e + b],
                                start=(cp == 0), stop=(cp == n_g - 1),
                                skip_group_check=True)
                        self.tc.inc(i)
                        assert self.tc.n == TPV[(g, cp)]
                cp = n_g - 1
                t.wait_ge(self.sc.sem, SEX[(g, cp)])
                offp = max(0, cp * 128 - qb)
                he = 2 * hp + e
                for (a, b) in pieces_of(offp):
                    i = t.matmul(PVp[0:65, a:b],
                                 self.v_tm[:, cp, he * 65:he * 65 + 65],
                                 self.p_buf[:, cp % 2,
                                            1024 * e + a:1024 * e + b],
                                 start=(cp == 0), stop=True,
                                 skip_group_check=True)
                self.tc.inc(i)
                assert self.tc.n == TPV[(g, cp)]
                t.wait_ge(self.vc.sem, VEV[g])
                for k in range(8):
                    i = t.transpose(S2[1][:, k * 128:(k + 1) * 128],
                                    self.ostg[:, k * 128:(k + 1) * 128],
                                    self.identf[:, :])
                self.tc.inc(i)
                assert self.tc.n == TTB[g]
                p = g - 1
                if 0 <= p < NP:
                    emit_trans_pair(t, p)
            for p in range(NG - 1, NP):
                emit_trans_pair(t, p)

        @block.scalar
        def _(s):
            for g, (q4, hp, e) in enumerate(groups):
                n_g = ngs[g]
                qb = 1024 * q4
                for c in range(n_g):
                    s.wait_ge(self.tc.sem, TSC[(g, c)])
                    if c >= 2:
                        s.wait_ge(self.tc.sem, TPV[(g, c - 2)])
                    elif g >= 2:
                        n2 = ngs[g - 2]
                        s.wait_ge(self.tc.sem, TPV[(g - 2, n2 - 2 + c)])
                    off = max(0, c * 128 - qb)
                    self.sc.inc(s.activation(
                        self.p_buf[:, c % 2, 1024 * e + off:1024 * e + 1024],
                        S2[c % 2][:, off:1024], AF.Exp,
                        bias=self.expb[:, hp:hp + 1], scale=0.125))
                    assert self.sc.n == SEX[(g, c)]

        @block.vector
        def _(v):
            for g in range(NG):
                v.wait_ge(self.tc.sem, TPV[(g, ngs[g] - 1)])
                self.vc.inc(v.tensor_copy(self.ostg[:, :], PVp[:, :]))
                assert self.vc.n == VEV[g]
                v.wait_ge(self.tc.sem, TTB[g])
                v.reciprocal(self.rden[:, 0:8], S2[1][:, 64:1024:128])
                v.drain()
                q4, hp, e = groups[g]
                he = 2 * hp + e
                for k in range(8):
                    i = v.tensor_scalar_mul(
                        self.o_all[:, 8 * q4 + k, he * 64:(he + 1) * 64],
                        S2[1][:, k * 128:k * 128 + 64],
                        self.rden[:, k:k + 1])
                self.vc.inc(i)
                assert self.vc.n == VNM[g]
                p = g - 1
                if 0 <= p < NP:
                    emit_evac_pair(v, p)
                if g < NP:
                    emit_deq_pair(v, g)
            for p in range(NG, NP + NG - 1):
                if p < NP:
                    emit_deq_pair(v, p)
                pe = p - 1
                if NG - 1 <= pe < NP:
                    emit_evac_pair(v, pe)

        @block.gpsimd
        def _(g_):
            def ag(q4):
                gl = 8 * (q4 + 1) - 1
                agi = self.g_ag1_in if q4 == 1 else self.g_ag2_in
                tlo = 0 if q4 == 1 else 16
                g_.wait_ge(self.vc.sem, VNM[gl])
                self.dG.inc(g_.dma_start(
                    out=agi.ap().rearrange("(t p) h -> p t h", p=128),
                    in_=self.o_all[:, tlo:tlo + 16, :]), 16)
                g_.wait_ge(self.dG.sem, self.dG.n)
                g_.collective_compute(
                    "AllGather", ALU.bypass,
                    replica_groups=[[0, 1], [2, 3], [4, 5], [6, 7]],
                    ins=[agi.ap().opt()],
                    outs=[self.g_ag_out.ap()[(q4 - 1) * TOWN:
                                             (q4 + 1) * TOWN, :].opt()],
                ).then_inc(self.cc.sem, 1)
                self.cc.n += 1

            for p in range(NP):
                for k in (2 * p - 4, 2 * p - 3):
                    if k >= 0:
                        g_.wait_ge(self.vc.sem, VET[k])
                        wt[k % 2].inc(g_.dma_start(out=dst_ap(k),
                                                   in_=wTst[:, k % 2, :, :]),
                                      16)
                for k in (2 * p, 2 * p + 1):
                    if p >= 2:
                        g_.wait_ge(self.vc.sem, VDQ[p - 2])
                    wl[k % 4].inc(g_.dma_start(out=wld[:, k % 4, :],
                                               in_=src_ap(k)), 16)
                if p == 17:
                    ag(1)
            for k in range(2 * NP - 4, 2 * NP):
                g_.wait_ge(self.vc.sem, VET[k])
                wt[k % 2].inc(g_.dma_start(out=dst_ap(k),
                                           in_=wTst[:, k % 2, :, :]), 16)
            ag(3)
        self.barrier()

    def p3(self, sc):
        block = self.block
        olu = self.sb(sc, "olu", [128, 2, 2, D], BF16)
        osel = self.sb(sc, "osel", [128, 2, D], BF16)
        ofm = self.sb(sc, "ofm", [128, 2, DCH, 128], BF16)
        xt3 = self.sb(sc, "xt3", [128, 2, D])
        xnw = self.sb(sc, "xnw", [128, 2, D])
        ln2t = self.sb(sc, "ln2t", [128, D])
        h2r = self.sb(sc, "h2r", [128, NT_OWN, D])
        h2q = self.sb(sc, "h2q", [128, 2, D], BF16)
        st3 = self.sb(sc, "st3", [128, 16])
        bn3 = self.sb(sc, "bn3", [128, 12])
        scrP = self.sb(sc, "scrP", [128, 128])
        p3_ps = ExitStack()
        pA = self.ps(p3_ps, "pA3", [128, 512])
        pB = self.ps(p3_ps, "pB3", [128, 512])
        pT3 = self.ps(p3_ps, "pT3", [128, 1024], BF16)
        pSc = self.ps(p3_ps, "pSc", [128, 128])
        x3 = [self.mkdma("x3a"), self.mkdma("x3b")]
        xo3 = [self.mkdma("xo3a"), self.mkdma("xo3b")]

        @block.sync
        def _(sync):
            sync.wait_ge(self.cc.sem, self.cc.n)
        self.barrier()

        V0, T0, S0 = self.vc.n, self.tc.n, self.sc.n

        def t_tr3(t):  return T0 + 3 * t + 1
        def t_op(t, w2): return T0 + 3 * t + 2 + w2
        def v_sel(t):  return V0 + 7 * t + 1
        def v_ofm(t):  return V0 + 7 * t + 2
        def v_epi(t, w2): return V0 + 7 * t + 3 + w2
        def v_st2(t):  return V0 + 7 * t + 5
        def v_h2(t):   return V0 + 7 * t + 7
        def s_sq2(t):  return S0 + t + 1
        G0p3 = self.gc.n

        def g_xga(t):  return G0p3 + t + 1



        olc = [self.mkdma("olc0"), self.mkdma("olc1")]

        @block.sync
        def _(sync):
            ago = self.g_ag_out.ap().rearrange("(t p) h -> p t h", p=128)
            xm = self.i_xmine.rearrange("(t p) d -> t p d", p=128)
            for t in range(NT_OWN):
                if t >= 2:
                    sync.wait_ge(self.vc.sem, v_sel(t - 2))
                olc[t % 2].inc(sync.dma_start(out=olu[:, t % 2, 0, 0:512],
                                              in_=ago[:, t, :]), 16)
                olc[t % 2].inc(sync.dma_start(out=olu[:, t % 2, 0, 512:1024],
                                              in_=ago[:, 16 + t, :]), 16)
                olc[t % 2].inc(sync.dma_start(out=olu[:, t % 2, 1, 0:512],
                                              in_=ago[:, 32 + t, :]), 16)
                olc[t % 2].inc(sync.dma_start(out=olu[:, t % 2, 1, 512:1024],
                                              in_=ago[:, 48 + t, :]), 16)
                if t >= 2:
                    sync.wait_ge(self.vc.sem, v_epi(t - 2, 1))
                x3[t % 2].inc(sync.dma_start(out=xt3[:, t % 2, :],
                                             in_=xm[t, :, :]), 16)
                if t >= 1:
                    sync.wait_ge(self.vc.sem, v_epi(t - 1, 1))
                    xo3[(t - 1) % 2].inc(sync.dma_start(
                        out=self.h_xnew[t - 1, :, :],
                        in_=xnw[:, (t - 1) % 2, :]), 16)
            sync.wait_ge(self.vc.sem, v_epi(NT_OWN - 1, 1))
            xo3[(NT_OWN - 1) % 2].inc(sync.dma_start(
                out=self.h_xnew[NT_OWN - 1, :, :],
                in_=xnw[:, (NT_OWN - 1) % 2, :]), 16)

        @block.gpsimd
        def _(g):
            for t in range(NT_OWN):
                g.wait_ge(x3[t % 2].sem, 16 * (t // 2 + 1))
                if t >= 2:
                    g.wait_ge(self.vc.sem, v_epi(t - 2, 1))
                self.gc.inc(g.tensor_tensor(xt3[:, t % 2, :],
                                            xt3[:, t % 2, :],
                                            self.gabo[:, :], ALU.add))
                assert self.gc.n == g_xga(t)

        @block.tensor
        def _(t_):
            for t in range(NT_OWN):
                t_.wait_ge(self.vc.sem, v_sel(t))
                if t >= 2:
                    t_.wait_ge(self.vc.sem, v_ofm(t - 2))
                for c in range(DCH):
                    i = t_.transpose(pT3[:, c * 128:(c + 1) * 128],
                                     osel[:, t % 2, c * 128:(c + 1) * 128],
                                     self.identb[:, :])
                self.tc.inc(i)
                assert self.tc.n == t_tr3(t)
                t_.wait_ge(self.vc.sem, v_ofm(t))
                for w2 in range(2):
                    if t >= 1:
                        t_.wait_ge(self.vc.sem, v_epi(t - 1, w2))
                    pq = pA if w2 == 0 else pB
                    for c in range(DCH):
                        i = t_.matmul(pq[:, :], ofm[:, t % 2, c, :],
                                      self.wo_s[:, c, w2 * 512:(w2 + 1) * 512],
                                      start=(c == 0), stop=(c == DCH - 1))
                    self.tc.inc(i)
                    assert self.tc.n == t_op(t, w2)

        @block.vector
        def _(v):
            for t in range(NT_OWN):
                v.wait_ge(olc[t % 2].sem, 16 * 4 * (t // 2 + 1))
                if t >= 2:
                    v.wait_ge(self.tc.sem, t_tr3(t - 2))
                v.tensor_tensor(olu[:, t % 2, 1, :], olu[:, t % 2, 1, :],
                                olu[:, t % 2, 0, :], ALU.subtract)
                v.drain()
                self.vc.inc(v.scalar_tensor_tensor(
                    osel[:, t % 2, :], olu[:, t % 2, 1, :],
                    self.scales[:, 14:15], olu[:, t % 2, 0, :],
                    ALU.mult, ALU.add))
                assert self.vc.n == v_sel(t)
                v.wait_ge(self.tc.sem, t_tr3(t))
                self.vc.inc(v.tensor_copy(
                    ofm[:, t % 2, :, :],
                    pT3[:, :].rearrange("p (c j) -> p c j", j=128)))
                assert self.vc.n == v_ofm(t)
                for w2 in range(2):
                    v.wait_ge(self.tc.sem, t_op(t, w2))
                    if w2 == 0:
                        v.wait_ge(self.gc.sem, g_xga(t))
                        if t >= 2:
                            v.wait_ge(xo3[t % 2].sem, 16 * (t // 2))
                    pq = pA if w2 == 0 else pB
                    sl = slice(w2 * 512, (w2 + 1) * 512)
                    v.tensor_tensor(ln2t[:, sl], pq[:, :],
                                    self.modb[:, 2, sl], ALU.mult)
                    v.drain()
                    self.vc.inc(v.tensor_tensor(xnw[:, t % 2, sl],
                                                ln2t[:, sl],
                                                xt3[:, t % 2, sl], ALU.add))
                    assert self.vc.n == v_epi(t, w2)
                v.drain()
                v.bn_stats(bn3[:, 0:6], xnw[:, t % 2, 0:512])
                v.bn_stats(bn3[:, 6:12], xnw[:, t % 2, 512:1024])
                v.drain()
                v.bn_aggr(st3[:, 0:2], bn3[:, :])
                self.vc.inc(enop(v))
                assert self.vc.n == v_st2(t)
                v.wait_ge(self.sc.sem, s_sq2(t))
                v.reciprocal(st3[:, 2:3], st3[:, 3:4])
                v.drain()
                v.tensor_scalar(ln2t[:, 0:D], xnw[:, t % 2, :],
                                st3[:, 0:1], st3[:, 2:3],
                                ALU.subtract, ALU.mult)
                v.drain()
                self.vc.inc(v.tensor_tensor(ln2t[:, 0:D], ln2t[:, 0:D],
                                self.modb[:, 0, :], ALU.mult))
                v.drain()
                v.tensor_tensor(h2r[:, t, :], ln2t[:, 0:D],
                                self.modb[:, 1, :], ALU.add)
                v.drain()
                v.tensor_reduce(st3[:, 4:5], h2r[:, t, :], AX.X,
                                ALU.max, apply_absolute_value=True)
                v.drain()
                if t == 0:
                    i = v.tensor_copy(st3[:, 5:6], st3[:, 4:5])
                else:
                    i = v.tensor_tensor(st3[:, 5:6], st3[:, 5:6],
                                        st3[:, 4:5], ALU.max)
                self.vc.inc(i)
                assert self.vc.n == v_h2(t)

        @block.scalar
        def _(s_):
            for t in range(NT_OWN):
                s_.wait_ge(self.vc.sem, v_st2(t))
                s_.activation(st3[:, 3:4], st3[:, 1:2], AF.Sqrt,
                              bias=self.small[:, 63:64], scale=1.0)
                self.sc.inc(enop(s_))
                assert self.sc.n == s_sq2(t)

        self.barrier()
        self.allreduce_scale([st3[:, 5:6]], scrP, pSc, 0, [0])

        V1, T1 = self.vc.n, self.tc.n

        @block.vector
        def _(v):
            for t in range(NT_OWN):
                if t >= 2:
                    v.wait_ge(self.tc.sem, T1 + (t - 1))
                v.tensor_scalar(h2r[:, t, :], h2r[:, t, :],
                                self.scales[:, 1:2], RC, ALU.mult, ALU.add)
                v.drain()
                self.vc.inc(v.tensor_scalar(
                    h2q[:, t % 2, :], h2r[:, t, :], RC,
                    self.scales[:, 0:1], ALU.subtract, ALU.mult))
                if t >= 1:
                    v.wait_ge(self.tc.sem, T1 + t)
                    tt = t - 1
                    self.vc.inc(v.tensor_copy(
                        self.h2fm[:, :, tt * 128:(tt + 1) * 128],
                        pT3[:, :].rearrange("p (c j) -> p c j", j=128)))
            v.wait_ge(self.tc.sem, T1 + NT_OWN)
            tt = NT_OWN - 1
            self.vc.inc(v.tensor_copy(
                self.h2fm[:, :, tt * 128:(tt + 1) * 128],
                pT3[:, :].rearrange("p (c j) -> p c j", j=128)))

        @block.tensor
        def _(t_):
            for t in range(NT_OWN):
                t_.wait_ge(self.vc.sem, V1 + 2 * t + 1)
                if t >= 1:
                    t_.wait_ge(self.vc.sem, V1 + 2 * t)
                for c in range(DCH):
                    i = t_.transpose(pT3[:, c * 128:(c + 1) * 128],
                                     h2q[:, t % 2, c * 128:(c + 1) * 128],
                                     self.identb[:, :])
                self.tc.inc(i)
        self.barrier()
        p3_ps.close()

    def allreduce_scale(self, colmax_aps, scrP, pSc, buf_i, slots):
        block = self.block
        N = len(colmax_aps)

        @block.vector
        def _(v):
            v.memset(scrP[:, :], 0.0)
            v.drain()
            for i, ap in enumerate(colmax_aps):
                v.tensor_copy(scrP[:, 32 * i:32 * i + 1], ap)
        self.barrier()

        @block.tensor
        def _(t):
            t.transpose(pSc[:, :], scrP[:, :], self.identf[:, :])
        self.barrier()

        @block.vector
        def _(v):
            for i in range(N):
                v.tensor_copy(self.row[0:1, 128 * i:128 * (i + 1)],
                              pSc[32 * i:32 * i + 1, :])
        self.barrier()

        @block.gpsimd
        def _(g):
            self.dG.inc(g.dma_start(out=self.g_ar_in[buf_i].ap()[0:1, 0:128 * N],
                                    in_=self.row[0:1, 0:128 * N]), 16)
            g.wait_ge(self.dG.sem, self.dG.n)
            g.collective_compute(
                "AllReduce", ALU.max, replica_groups=[list(range(8))],
                ins=[self.g_ar_in[buf_i].ap()[0:1, 0:128 * N].opt()],
                outs=[self.g_ar_out[buf_i].ap()[0:1, 0:128 * N].opt()],
            ).then_inc(self.cc.sem, 1)
            self.cc.n += 1
            g.wait_ge(self.cc.sem, self.cc.n)
            self.dG.inc(g.dma_start(out=self.row[0:1, 0:128 * N],
                                    in_=self.g_ar_out[buf_i].ap()[0:1, 0:128 * N]),
                        16)
        self.barrier()

        @block.vector
        def _(v):
            for i in range(N):
                v.tensor_reduce(self.row[0:1, 512 + 2 * i:513 + 2 * i],
                                self.row[0:1, 128 * i:128 * (i + 1)],
                                AX.X, ALU.max)
            v.drain()
            for i in range(N):
                v.tensor_scalar(self.row[0:1, 512 + 2 * i:513 + 2 * i],
                                self.row[0:1, 512 + 2 * i:513 + 2 * i],
                                INV15, 1e-12, ALU.mult, ALU.max)
            v.drain()
            for i in range(N):
                v.reciprocal(self.row[0:1, 513 + 2 * i:514 + 2 * i],
                             self.row[0:1, 512 + 2 * i:513 + 2 * i])
        self.barrier()

        @block.tensor
        def _(t):
            t.matmul(pSc[:, 0:2 * N], self.onesf[0:1, :],
                     self.row[0:1, 512:512 + 2 * N], start=True, stop=True)
        self.barrier()

        @block.vector
        def _(v):
            for i, slot in enumerate(slots):
                v.tensor_copy(self.scales[:, 2 * slot:2 * slot + 2],
                              pSc[:, 2 * i:2 * i + 2])
        self.barrier()


    def p4(self, sc):
        block = self.block
        wsg = self.sb(sc, "wsg", [128, 2, DCH, 128], BF16)
        wsu = self.sb(sc, "wsu", [128, 2, DCH, 128], BF16)
        gstg = self.sb(sc, "gstg", [128, 2, TOWN])
        ustg = self.sb(sc, "ustg", [128, 2, TOWN])
        mx4 = self.sb(sc, "mx4", [128, 200])
        scrP = self.sb(sc, "scrP4", [128, 128])
        p4_ps = ExitStack()
        pG = [self.ps(p4_ps, "pG0", [128, 512]), self.ps(p4_ps, "pG1", [128, 512])]
        pU = [self.ps(p4_ps, "pU0", [128, 512]), self.ps(p4_ps, "pU1", [128, 512])]
        pSc = self.ps(p4_ps, "pSc4", [128, 128])
        wg4 = [self.mkdma("wg4a"), self.mkdma("wg4b")]
        wu4 = [self.mkdma("wu4a"), self.mkdma("wu4b")]
        sp4 = [self.mkdma("sp4a"), self.mkdma("sp4b")]

        V0, T0, S0 = self.vc.n, self.tc.n, self.sc.n

        def t_mm(ic, win, gu): return T0 + 8 * ic + 2 * win + gu + 1
        def v_gev(ic, win):    return V0 + 8 * ic + 2 * win + 1
        def v_urd(ic, win):    return V0 + 8 * ic + 2 * win + 2
        def s_uev(ic, win):    return S0 + 4 * ic + win + 1

        @block.gpsimd
        def _(g):
            for ic in range(ICH):
                if ic >= 2:
                    g.wait_ge(self.tc.sem, t_mm(ic - 1, 3, 1))
                wg4[ic % 2].inc(g.dma_start(
                    out=wsg[:, ic % 2, :, :],
                    in_=self.h_wg[:, ic, :, :].rearrange("d p j -> p d j")), 16)
                wu4[ic % 2].inc(g.dma_start(
                    out=wsu[:, ic % 2, :, :],
                    in_=self.h_wu[:, ic, :, :].rearrange("d p j -> p d j")), 16)
                if ic >= 2:
                    icp = ic - 2
                    g.wait_ge(self.vc.sem, v_gev(icp, 3))
                    sp4[icp % 2].inc(g.dma_start(out=self.h_gate[icp, :, :],
                                                 in_=gstg[:, icp % 2, :]), 16)
                    g.wait_ge(self.vc.sem, v_urd(icp, 3))
                    sp4[icp % 2].inc(g.dma_start(out=self.h_up[icp, :, :],
                                                 in_=ustg[:, icp % 2, :]), 16)
            for icp in (ICH - 2, ICH - 1):
                g.wait_ge(self.vc.sem, v_gev(icp, 3))
                sp4[icp % 2].inc(g.dma_start(out=self.h_gate[icp, :, :],
                                             in_=gstg[:, icp % 2, :]), 16)
                g.wait_ge(self.vc.sem, v_urd(icp, 3))
                sp4[icp % 2].inc(g.dma_start(out=self.h_up[icp, :, :],
                                             in_=ustg[:, icp % 2, :]), 16)

        @block.tensor
        def _(t):
            for ic in range(ICH):
                t.wait_ge(wg4[ic % 2].sem, 16 * (ic // 2 + 1))
                t.wait_ge(wu4[ic % 2].sem, 16 * (ic // 2 + 1))
                for win in range(4):
                    sl = slice(win * 512, (win + 1) * 512)
                    pic, pwin = (ic, win - 2) if win >= 2 else (ic - 1, win + 2)
                    if pic >= 0:
                        t.wait_ge(self.vc.sem, v_gev(pic, pwin))
                    for c in range(DCH):
                        i = t.matmul(pG[win % 2][:, :], wsg[:, ic % 2, c, :],
                                     self.h2fm[:, c, sl],
                                     start=(c == 0), stop=(c == DCH - 1))
                    self.tc.inc(i)
                    assert self.tc.n == t_mm(ic, win, 0)
                    if pic >= 0:
                        t.wait_ge(self.sc.sem, s_uev(pic, pwin))
                        t.wait_ge(self.vc.sem, v_urd(pic, pwin))
                    for c in range(DCH):
                        i = t.matmul(pU[win % 2][:, :], wsu[:, ic % 2, c, :],
                                     self.h2fm[:, c, sl],
                                     start=(c == 0), stop=(c == DCH - 1))
                    self.tc.inc(i)
                    assert self.tc.n == t_mm(ic, win, 1)

        @block.vector
        def _(v):
            for ic in range(ICH):
                for win in range(4):
                    sl = slice(win * 512, (win + 1) * 512)
                    v.wait_ge(self.tc.sem, t_mm(ic, win, 0))
                    if ic >= 2 and win == 0:
                        v.wait_ge(sp4[ic % 2].sem, 32 * (ic // 2))
                    kc = 4 * ic + win
                    v.tensor_copy(gstg[:, ic % 2, sl], pG[win % 2][:, :])
                    self.vc.inc(v.tensor_reduce(mx4[:, kc:kc + 1],
                                                pG[win % 2][:, :], AX.X,
                                                ALU.max,
                                                apply_absolute_value=True))
                    assert self.vc.n == v_gev(ic, win)
                    v.wait_ge(self.sc.sem, s_uev(ic, win))
                    self.vc.inc(v.tensor_reduce(mx4[:, 100 + kc:101 + kc],
                                                pU[win % 2][:, :], AX.X,
                                                ALU.max,
                                                apply_absolute_value=True))
                    assert self.vc.n == v_urd(ic, win)

        @block.scalar
        def _(s):
            for ic in range(ICH):
                for win in range(4):
                    sl = slice(win * 512, (win + 1) * 512)
                    s.wait_ge(self.tc.sem, t_mm(ic, win, 1))
                    if ic >= 2 and win == 0:
                        s.wait_ge(sp4[ic % 2].sem, 32 * (ic // 2))
                    s.copy(ustg[:, ic % 2, sl], pU[win % 2][:, :])
                    self.sc.inc(enop(s))
                    assert self.sc.n == s_uev(ic, win)

        @block.vector
        def _(v):
            v.drain()
            v.tensor_reduce(mx4[:, 196:197], mx4[:, 0:96], AX.X, ALU.max)
            v.tensor_reduce(mx4[:, 197:198], mx4[:, 100:196], AX.X, ALU.max)
        self.barrier()
        self.allreduce_scale([mx4[:, 196:197], mx4[:, 197:198]], scrP, pSc,
                             1, [1, 2])
        p4_ps.close()

    def p56(self, sc):
        block = self.block
        self.qg = self.sb(sc, "qg", [128, ICH, TOWN], I16)
        grd = self.sb(sc, "grd", [128, 3, TOWN])
        sgf = self.sb(sc, "sgf", [128, 2, TOWN])
        tf = self.sb(sc, "tf", [128, TOWN])
        osp = self.sb(sc, "osp", [128, 2, TOWN])
        mx5 = self.sb(sc, "mx5", [128, ICH + 2])
        scrP = self.sb(sc, "scrP5", [128, 128])
        p5_ps = ExitStack()
        pSc = self.ps(p5_ps, "pSc5", [128, 128])
        g5 = [self.mkdma("g5a"), self.mkdma("g5b"), self.mkdma("g5c")]
        o5 = [self.mkdma("o5a"), self.mkdma("o5b")]

        sG, rG = self.scales[:, 2:3], self.scales[:, 3:4]
        sU, rU = self.scales[:, 4:5], self.scales[:, 5:6]
        s3, r3p = self.scales[:, 6:7], self.scales[:, 10:11]

        S0, V0 = self.sc.n, self.vc.n
        sA, sB, vA, vB = {}, {}, {}, {}
        cnt = S0
        for cyc in range(ICH + 1):
            if cyc < ICH:
                cnt += 1; sA[cyc] = cnt
            if cyc >= 1:
                cnt += 1; sB[cyc - 1] = cnt
        cnt = V0
        for cyc in range(ICH + 2):
            if cyc < ICH:
                cnt += 1; vA[cyc] = cnt
            if cyc >= 2:
                cnt += 1; vB[cyc - 2] = cnt

        @block.sync
        def _(sync):
            for k in range(ICH):
                if k >= 3:
                    sync.wait_ge(self.vc.sem, vA[k - 3])
                g5[k % 3].inc(sync.dma_start(out=grd[:, k % 3, :],
                                             in_=self.h_gate[k, :, :]), 16)

        @block.scalar
        def _(s):
            for cyc in range(ICH + 1):
                if cyc < ICH:
                    s.wait_ge(g5[cyc % 3].sem, 16 * (cyc // 3 + 1))
                    self.sc.inc(s.activation(grd[:, cyc % 3, :],
                                             grd[:, cyc % 3, :], AF.Copy,
                                             bias=RC, scale=rG))
                    assert self.sc.n == sA[cyc]
                if cyc >= 1:
                    k = cyc - 1
                    s.wait_ge(self.vc.sem, vB[k - 2] if k >= 2 else vA[k])
                    s.activation(sgf[:, k % 2, :], self.qg[:, k, :],
                                 AF.Sigmoid, scale=sG)
                    s.drain()
                    self.sc.inc(s.activation(sgf[:, k % 2, :],
                                             sgf[:, k % 2, :], AF.Copy,
                                             bias=RC, scale=65536.0))
                    assert self.sc.n == sB[k]

        @block.vector
        def _(v):
            for cyc in range(ICH + 2):
                if cyc < ICH:
                    v.wait_ge(self.sc.sem, sA[cyc])
                    self.vc.inc(v.tensor_scalar(self.qg[:, cyc, :],
                                                grd[:, cyc % 3, :],
                                                RC, None, ALU.subtract))
                    assert self.vc.n == vA[cyc]
                if cyc >= 2:
                    k = cyc - 2
                    v.wait_ge(self.sc.sem, sB[k])
                    v.tensor_scalar(sgf[:, k % 2, :], sgf[:, k % 2, :],
                                    RC, 65535.0, ALU.subtract, ALU.min)
                    v.drain()
                    v.scalar_tensor_tensor(
                        tf[:, :], self.qg[:, k, :], sG,
                        sgf[:, k % 2, :], ALU.mult, ALU.mult)
                    v.drain()
                    self.vc.inc(v.tensor_reduce(
                        mx5[:, k:k + 1], tf[:, :], AX.X, ALU.max,
                        apply_absolute_value=True))
                    assert self.vc.n == vB[k]
            v.drain()
            v.tensor_reduce(mx5[:, ICH:ICH + 1], mx5[:, 0:ICH], AX.X, ALU.max)
            v.drain()
            v.tensor_scalar_mul(mx5[:, ICH:ICH + 1], mx5[:, ICH:ICH + 1],
                                1.0 / 65536.0)

        self.barrier()
        self.allreduce_scale([mx5[:, ICH:ICH + 1]], scrP, pSc, 2, [3])

        @block.vector
        def _(v):
            v.tensor_scalar(self.scales[:, 10:11], self.scales[:, 7:8],
                            1.0 / 65536.0, None, ALU.mult)
        self.barrier()

        S1, V1 = self.sc.n, self.vc.n
        sA1, sA2, sB2, vA2, vB2 = {}, {}, {}, {}, {}
        cnt = S1
        for cyc in range(ICH + 2):
            if cyc >= 2:
                cnt += 1; sB2[cyc - 2] = cnt
            if cyc < ICH:
                cnt += 1; sA1[cyc] = cnt
                cnt += 1; sA2[cyc] = cnt
        cnt = V1
        for cyc in range(ICH + 2):
            if cyc >= 2:
                cnt += 1; vB2[cyc - 2] = cnt
            if cyc >= 1 and cyc <= ICH:
                cnt += 1; vA2[cyc - 1] = cnt

        @block.sync
        def _(sync):
            for k in range(ICH):
                if k >= 3:
                    sync.wait_ge(self.vc.sem, vB2[k - 3])
                    o5[(k - 3) % 2].inc(sync.dma_start(
                        out=self.h_o[k - 3, :, :],
                        in_=osp[:, (k - 3) % 2, :]), 16)
                g5[k % 3].inc(sync.dma_start(out=grd[:, k % 3, :],
                                             in_=self.h_up[k, :, :]), 16)
            for k in range(ICH - 3, ICH):
                sync.wait_ge(self.vc.sem, vB2[k])
                o5[k % 2].inc(sync.dma_start(out=self.h_o[k, :, :],
                                             in_=osp[:, k % 2, :]), 16)

        @block.scalar
        def _(s):
            for cyc in range(ICH + 2):
                if cyc >= 2:
                    k = cyc - 2
                    s.wait_ge(self.vc.sem, vA2[k])
                    self.sc.inc(s.activation(tf[:, :], tf[:, :],
                                             AF.Copy, bias=RC, scale=r3p))
                    assert self.sc.n == sB2[k]
                if cyc < ICH:
                    s.wait_ge(g5[cyc % 3].sem,
                              16 * (ICH // 3 + cyc // 3 + 1))
                    self.sc.inc(s.activation(grd[:, cyc % 3, :],
                                             grd[:, cyc % 3, :], AF.Copy,
                                             bias=RC, scale=rU))
                    assert self.sc.n == sA1[cyc]
                    s.activation(sgf[:, cyc % 2, :], self.qg[:, cyc, :],
                                 AF.Sigmoid, scale=sG)
                    s.drain()
                    self.sc.inc(s.activation(sgf[:, cyc % 2, :],
                                             sgf[:, cyc % 2, :], AF.Copy,
                                             bias=RC, scale=65536.0))
                    assert self.sc.n == sA2[cyc]

        @block.vector
        def _(v):
            for cyc in range(ICH + 2):
                if cyc >= 2:
                    k = cyc - 2
                    v.wait_ge(self.sc.sem, sB2[k])
                    v.tensor_scalar(self.qg[:, k, :], tf[:, :],
                                    RC, None, ALU.subtract)
                    v.drain()
                    if k >= 2:
                        v.wait_ge(o5[k % 2].sem, 16 * (k // 2))
                    v.scalar_tensor_tensor(
                        osp[:, k % 2, :], self.qg[:, k, :], s3,
                        grd[:, k % 3, :], ALU.mult, ALU.mult)
                    v.drain()
                    self.vc.inc(v.tensor_reduce(
                        mx5[:, k:k + 1], osp[:, k % 2, :], AX.X, ALU.max,
                        apply_absolute_value=True))
                    assert self.vc.n == vB2[k]
                if cyc >= 1 and cyc <= ICH:
                    k1 = cyc - 1
                    v.wait_ge(self.sc.sem, sA1[k1])
                    v.tensor_scalar(grd[:, k1 % 3, :], grd[:, k1 % 3, :],
                                    RC, sU, ALU.subtract, ALU.mult)
                    v.wait_ge(self.sc.sem, sA2[k1])
                    v.tensor_scalar(sgf[:, k1 % 2, :], sgf[:, k1 % 2, :],
                                    RC, 65535.0, ALU.subtract, ALU.min)
                    v.drain()
                    self.vc.inc(v.scalar_tensor_tensor(
                        tf[:, :], self.qg[:, k1, :], sG,
                        sgf[:, k1 % 2, :], ALU.mult, ALU.mult))
                    assert self.vc.n == vA2[k1]
            v.drain()
            v.tensor_reduce(mx5[:, ICH + 1:ICH + 2], mx5[:, 0:ICH], AX.X,
                            ALU.max)

        self.barrier()
        self.allreduce_scale([mx5[:, ICH + 1:ICH + 2]], scrP, pSc, 3, [4])
        p5_ps.close()

    def p7(self, sc):
        block = self.block
        OBR = ICH + 2
        obf = self.sb(sc, "obf", [128, OBR, 512], BF16)
        oin = self.sb(sc, "oin", [128, 4, 512])
        ofq = self.sb(sc, "ofq", [128, 2, 512])
        wdst = self.sb(sc, "wdst", [128, ICH, DCH, 128], BF16)
        xnw3 = self.sb(sc, "xnw3", [128, 4, D])
        res3 = self.sb(sc, "res3", [128, 2, D])
        p7_ps = ExitStack()
        pD = [self.ps(p7_ps, f"pD{j}", [128, 512]) for j in range(8)]
        oi = [self.mkdma(f"oi{i}") for i in range(4)]
        w7 = [self.mkdma(f"w7{i}") for i in range(4)]
        x7 = [self.mkdma(f"x7{i}") for i in range(4)]
        o7 = [self.mkdma("o7a"), self.mkdma("o7b")]

        sO, rO = self.scales[:, 8:9], self.scales[:, 9:10]

        V0, T0, S0 = self.vc.n, self.tc.n, self.sc.n
        voq, vepi, sq1 = {}, {}, {}
        cnt = V0
        for w in range(4):
            for ic in range(ICH):
                cnt += 1
                voq[(w, ic)] = cnt
            if w >= 1:
                for j4 in range(4):
                    cnt += 1
                    vepi[(w - 1, j4)] = cnt
        for j4 in range(4):
            cnt += 1
            vepi[(3, j4)] = cnt
        cnt = S0
        for w in range(4):
            for ic in range(ICH):
                cnt += 1
                sq1[(w, ic)] = cnt

        def t7(w, ic):  return T0 + ICH * w + ic + 1

        @block.gpsimd
        def _(g):
            for ic in range(ICH):
                if ic >= 4:
                    g.wait_ge(self.tc.sem, t7(0, ic - 4))
                w7[ic % 4].inc(g.dma_start(
                    out=wdst[:, ic, :, :],
                    in_=self.h_wd[ic, :, :, :].rearrange(
                        "d p j -> p d j")), 16)

        @block.sync
        def _(sync):
            for w in range(4):
                for ic in range(ICH):
                    gi = w * ICH + ic
                    if gi >= 4:
                        sync.wait_ge(self.sc.sem,
                                     sq1[divmod(gi - 4, ICH)])
                    oi[gi % 4].inc(sync.dma_start(
                        out=oin[:, gi % 4, :],
                        in_=self.h_o[ic, :, w * 512:(w + 1) * 512]), 16)
                for j4 in range(4):
                    t = w * 4 + j4
                    if t >= 4:
                        sync.wait_ge(self.vc.sem, vepi[((t - 4) // 4,
                                                        (t - 4) % 4)])
                    x7[t % 4].inc(sync.dma_start(
                        out=xnw3[:, t % 4, :],
                        in_=self.h_xnew[t, :, :]), 16)

        @block.scalar
        def _(s):
            for w in range(4):
                for ic in range(ICH):
                    gi = w * ICH + ic
                    s.wait_ge(oi[gi % 4].sem, 16 * (gi // 4 + 1))
                    if gi >= 2:
                        s.wait_ge(self.vc.sem, voq[divmod(gi - 2, ICH)])
                    self.sc.inc(s.activation(ofq[:, gi % 2, :],
                                             oin[:, gi % 4, :], AF.Copy,
                                             bias=RC, scale=rO))
                    assert self.sc.n == sq1[(w, ic)]
                if w >= 1:
                    for j4 in range(4):
                        t = (w - 1) * 4 + j4
                        s.wait_ge(self.vc.sem, vepi[(w - 1, j4)])
                        o7[t % 2].inc(s.dma_start(out=self.o_out[t, :, :],
                                                  in_=res3[:, t % 2, :]), 16)
            for j4 in range(4):
                t = 12 + j4
                s.wait_ge(self.vc.sem, vepi[(3, j4)])
                o7[t % 2].inc(s.dma_start(out=self.o_out[t, :, :],
                                          in_=res3[:, t % 2, :]), 16)

        @block.vector
        def _(v):
            def emit_oq(w, ic):
                gi = w * ICH + ic
                v.wait_ge(self.sc.sem, sq1[(w, ic)])
                if gi >= OBR:
                    pw, pic = divmod(gi - OBR, ICH)
                    v.wait_ge(self.tc.sem, t7(pw, pic))
                self.vc.inc(v.tensor_scalar(obf[:, gi % OBR, :],
                                            ofq[:, gi % 2, :], RC, sO,
                                            ALU.subtract, ALU.mult))
                assert self.vc.n == voq[(w, ic)]

            def emit_epi(w):
                for j4 in range(4):
                    t = w * 4 + j4
                    v.wait_ge(self.tc.sem, t7(w, ICH - 1))
                    v.wait_ge(x7[t % 4].sem, 16 * (t // 4 + 1))
                    if t >= 2:
                        v.wait_ge(o7[t % 2].sem, 16 * (t // 2))
                    for dw2 in range(2):
                        dsl = slice(dw2 * 512, (dw2 + 1) * 512)
                        v.tensor_tensor(res3[:, t % 2, dsl],
                                        pD[4 * dw2 + j4][:, :],
                                        self.modb[:, 3, dsl], ALU.mult)
                        v.drain()
                        v.tensor_tensor(res3[:, t % 2, dsl],
                                        res3[:, t % 2, dsl],
                                        xnw3[:, t % 4, dsl], ALU.add)
                    self.vc.inc(enop(v))
                    assert self.vc.n == vepi[(w, j4)]

            for w in range(4):
                for ic in range(ICH):
                    emit_oq(w, ic)
                if w >= 1:
                    emit_epi(w - 1)
            emit_epi(3)

        @block.tensor
        def _(t_):
            for w in range(4):
                for ic in range(ICH):
                    t_.wait_ge(self.vc.sem, voq[(w, ic)])
                    t_.wait_ge(w7[ic % 4].sem, 16 * (ic // 4 + 1))
                    if ic == 0 and w >= 1:
                        t_.wait_ge(self.vc.sem, vepi[(w - 1, 3)])
                    gi = w * ICH + ic
                    for j4 in range(4):
                        for dw2 in range(2):
                            t_.matmul(pD[4 * dw2 + j4][:, :],
                                      obf[:, gi % OBR, j4 * 128:(j4 + 1) * 128],
                                      wdst[:, ic, dw2 * 4:dw2 * 4 + 4, :]
                                      .rearrange("p a b -> p (a b)"),
                                      start=(ic == 0), stop=(ic == ICH - 1))
                    self.tc.inc(enop(t_))
                    assert self.tc.n == t7(w, ic)

        self.barrier()
        p7_ps.close()

    def build(self):
        self.declare_io()
        ctx = self.ctx
        perm_sc = ctx.enter_context(ExitStack())
        self.alloc_perm(perm_sc)

        attn_sc = ExitStack()
        self.q_fm = self.sb(attn_sc, "q_fm", [128, MCH, S], BF16)
        self.k_fm = self.sb(attn_sc, "k_fm", [128, MCH, S], BF16)
        self.v_tm = self.sb(attn_sc, "v_tm", [128, NT_ALL, 8 * 65], BF16)

        qkv_sc = ExitStack()
        self.wq_s = self.sb(qkv_sc, "wq_s", [128, DCH, 512], BF16)
        self.wk_s = self.sb(qkv_sc, "wk_s", [128, DCH, 512], BF16)
        self.wv_s = self.sb(qkv_sc, "wv_s", [128, DCH, 512], BF16)
        self.hfm = self.sb(qkv_sc, "hfm", [128, 2, DCH, 512], BF16)
        self.xst = self.sb(qkv_sc, "xst", [128, 2, D])
        self.ht = self.sb(qkv_sc, "ht", [128, 2, D], BF16)
        self.sqb = self.sb(qkv_sc, "sqb", [128, 512], BF16)
        self.stats = self.sb(qkv_sc, "stats", [128, 16])
        self.bn6 = self.sb(qkv_sc, "bn6", [128, 12])
        self.rows4 = self.sb(qkv_sc, "rows4", [1, 8, 128])
        self.scr128 = self.sb(qkv_sc, "scr128", [128, 128])
        self.qnst = self.sb(qkv_sc, "qnst", [2, 8, 8])

        qkv_ps_sc = ExitStack()
        self.psA = self.ps(qkv_ps_sc, "psA", [128, 512])
        self.psB = self.ps(qkv_ps_sc, "psB", [128, 512])
        self.psC = self.ps(qkv_ps_sc, "psC", [128, 512])
        self.psD = self.ps(qkv_ps_sc, "psD", [128, 512])
        self.psN = self.ps(qkv_ps_sc, "psN", [2, 512])
        self.psT = self.ps(qkv_ps_sc, "psT", [128, 2, 1024], BF16)
        self.qkv_ps = [self.psA, self.psB, self.psC, self.psD]

        with ExitStack() as p0_sc:
            self.p0(p0_sc)
        self.p1()
        self.p1b()

        self.dbg_dump("qfm", self.q_fm[:, :, :], [128, MCH, S], BF16)
        self.dbg_dump("kfm", self.k_fm[:, :, :], [128, MCH, S], BF16)
        self.dbg_dump("vtm", self.v_tm[:, :, :], [128, NT_ALL, 8 * 65], BF16)
        self.dbg_dump("modfm", self.mod_fm[:, 0:48], [128, 48])
        self.dbg_dump("modb", self.modb[:, :, :], [128, 4, D])
        self.dbg_dump("gabo", self.gabo[:, :], [128, D])
        self.dbg_dump("expb", self.expb[:, :], [128, 4])

        qkv_sc.close()
        qkv_ps_sc.close()

        self.o_all = self.sb(attn_sc, "o_all", [128, NT_ALL, 512], BF16)
        self.ostg = self.sb(attn_sc, "ostg", [128, 1024])
        self.rden = self.sb(attn_sc, "rden", [128, 8])
        self.p_buf = self.sb(attn_sc, "p_buf", [128, 2, 2048], BF16)
        p2m_sc = ExitStack()
        self.wld = self.sb(p2m_sc, "wld", [128, 4, 1024])
        self.bm = self.sb(p2m_sc, "bm", [128, 2, 96])
        self.deqc = self.sb(p2m_sc, "deqc", [128, 2, 1024], BF16)
        self.wTst = self.sb(p2m_sc, "wTst", [128, 2, 8, 128], BF16)
        p2_ps = ExitStack()
        self.S2a = self.ps(p2_ps, "S2a", [128, 1024])
        self.S2b = self.ps(p2_ps, "S2b", [128, 1024])
        self.PVp = self.ps(p2_ps, "PVp", [128, 1024])
        self.ptx0 = self.ps(p2_ps, "ptx0", [128, 1024], BF16)
        self.ptx1 = self.ps(p2_ps, "ptx1", [128, 1024], BF16)
        self.p2(NP=36, NCH=72)
        self.dbg_dump("oall", self.o_all[:, :, :], [128, NT_ALL, 512], BF16)
        p2_ps.close()
        p2m_sc.close()
        attn_sc.close()

        p34_sc = ExitStack()
        self.h2fm = self.sb(p34_sc, "h2fm", [128, DCH, TOWN], BF16)
        self.wo_s = self.sb(p34_sc, "wo_s", [128, DCH, D], BF16)

        @self.block.gpsimd
        def _(g):
            wor = self.i_wo.rearrange("(c p) m -> c p m", p=128)
            for c in range(DCH):
                self.dG.inc(g.dma_start(out=self.wo_s[:, c, :],
                                        in_=wor[c, :, :]), 16)
        p3_sc = ExitStack()
        self.p3(p3_sc)
        self.dbg_dump("h2fm", self.h2fm[:, :, :], [128, DCH, TOWN], BF16)
        self.dbg_dump("scl", self.scales[:, :], [128, 16])
        p3_sc.close()

        p4_sc = ExitStack()
        self.p4(p4_sc)
        p4_sc.close()
        p34_sc.close()

        p56_sc = ExitStack()
        self.p56(p56_sc)
        self.dbg_dump("scl2", self.scales[:, :], [128, 16])
        self.dbg_dump("qgd", self.qg[:, :, :], [128, ICH, TOWN], I16)
        p56_sc.close()

        p7_sc = ExitStack()
        self.p7(p7_sc)
        p7_sc.close()

        self.barrier()



def _consts():
    identb = np.eye(128, dtype=ml_dtypes.bfloat16)
    identf = np.eye(128, dtype=np.float32)
    m = np.where(np.arange(128)[None, :] >= np.arange(128)[:, None],
                 np.float32(0.0), np.float32(-1e30))
    maskb = m.astype(ml_dtypes.bfloat16)
    onesf = np.ones((1, 128), np.float32)
    onesb = np.ones((1, 128), ml_dtypes.bfloat16)
    hselb = np.zeros((128, 2), ml_dtypes.bfloat16)
    hselb[0:64, 0] = 1
    hselb[64:128, 1] = 1
    eself = np.zeros((64, 1024), np.float32)
    for kk in range(64):
        for b in range(32):
            if kk % 32 == b:
                eself[kk, 32 * b:32 * b + 32] = 1.0
    return identb, identf, maskb, onesf, onesb, hselb, eself


def stage_inputs(inputs):
    x = np.asarray(inputs["L0_attn_x"], np.float32)
    wq = np.asarray(inputs["L0_attn_wq"], np.float32)
    wk = np.asarray(inputs["L0_attn_wk"], np.float32)
    wv = np.asarray(inputs["L0_attn_wv"], np.float32)
    bq = np.asarray(inputs["L0_attn_bq"], np.float32).reshape(1, D)
    bk = np.asarray(inputs["L0_attn_bk"], np.float32).reshape(1, D)
    bv = np.asarray(inputs["L0_attn_bv"], np.float32).reshape(1, D)
    wo = np.asarray(inputs["L0_attn_wo"], np.float32)
    bo = np.asarray(inputs["L0_attn_bo"], np.float32).reshape(1, D)
    wu = np.asarray(inputs["L0_ffn_w_up"], np.float32)
    wg = np.asarray(inputs["L0_ffn_w_gate"], np.float32)
    wd = np.asarray(inputs["L0_ffn_w_down"], np.float32)
    cc = np.asarray(inputs["L0_c"], np.float32)
    wmod = np.asarray(inputs["L0_wmod"], np.float32)
    identb, identf, maskb, onesf, onesb, hselb, eself = _consts()

    in_maps = []
    for core in range(8):
        b, half = core // 2, core % 2
        hsl = slice(half * 512, half * 512 + 512)
        in_maps.append({
            "xfull": np.ascontiguousarray(x[b]),
            "xmine": np.ascontiguousarray(x[b, half * TOWN:(half + 1) * TOWN]),
            "wq": np.ascontiguousarray(wq[:, hsl]),
            "wk": np.ascontiguousarray(wk[:, hsl]),
            "wv": np.ascontiguousarray(wv[:, hsl]),
            "bq": np.ascontiguousarray(bq[:, hsl]),
            "bk": np.ascontiguousarray(bk[:, hsl]),
            "bv": np.ascontiguousarray(bv[:, hsl]),
            "wo": wo, "bo": bo, "wu": wu, "wg": wg, "wd": wd,
            "crow": np.ascontiguousarray(cc[b:b + 1]),
            "wmod": wmod,
            "identb": identb, "identf": identf, "maskb": maskb,
            "onesf": onesf, "onesb": onesb, "hselb": hselb, "eself": eself,
            "halfidx": np.array([[half, 1 - half]], np.int32),
        })
    return in_maps


def build(dbg=()):
    nc = bass.Bass("TRN2", target_bir_lowering=False, debug=False)
    with ExitStack() as ctx:
        with nc.Block() as block:
            kb = KB(nc, ctx, block, dbg)
            kb.build()
    return nc, kb


def kernel(**inputs):
    nc, kb = build()
    in_maps = stage_inputs(inputs)
    res = run_bass_kernel_spmd(nc, in_maps, core_ids=list(range(8)))
    out = np.zeros((B, S, D), np.float32)
    for core in range(8):
        b, half = core // 2, core % 2
        out[b, half * TOWN:(half + 1) * TOWN] = \
            res.results[core]["out"].reshape(TOWN, D)
    return out


# version 18 (final)
# speedup vs baseline: 1.0614x; 1.0614x over previous
import numpy as np
import ml_dtypes
from contextlib import ExitStack

import concourse.bass as bass
import concourse.mybir as mybir
from concourse.bass_utils import run_bass_kernel_spmd

F32 = mybir.dt.float32
BF16 = mybir.dt.bfloat16
I16 = mybir.dt.int16
I32 = mybir.dt.int32
AF = mybir.ActivationFunctionType
ALU = mybir.AluOpType
AX = mybir.AxisListType
ds = bass.ds

B = 4
D = 1024
S = 4096
NHC = 8
HD = 64
I3 = 3072
TOWN = 2048
NT_OWN = TOWN // 128
NT_ALL = S // 128
DCH = D // 128
ICH = I3 // 128
MCH = 4

RC = float(np.float32(1.5 * 2 ** 23))
INV7 = float(np.float32(1.0 / 7.0))
INV15 = float(np.float32(1.0 / 32767.0))
LN_EPS = 1e-5

ENGS = ("sync", "vector", "scalar", "tensor", "gpsimd")


def enop(eng):
    return eng.drain()


class Chan:
    def __init__(self, nc, ctx, name):
        self.sem = ctx.enter_context(nc.semaphore(name))
        self.n = 0

    def inc(self, instr, k=1):
        instr.then_inc(self.sem, k)
        self.n += k
        return self.n

    def wait(self, eng, n=None):
        eng.wait_ge(self.sem, self.n if n is None else n)


class KB:
    def __init__(self, nc, ctx, block, dbg=()):
        self.nc = nc
        self.ctx = ctx
        self.block = block
        self.dbg = set(dbg)
        self.dbg_t = {}
        ch = lambda name: Chan(nc, ctx, name)
        self.dS = ch("dS")
        self.dG = ch("dG")
        self.cc = ch("cc")
        self.tc = ch("tc")
        self.vc = ch("vc")
        self.sc = ch("sc")
        self.gc = ch("gc")
        self.yc = ch("yc")
        self.chans = {"sync": self.yc, "vector": self.vc, "scalar": self.sc,
                      "tensor": self.tc, "gpsimd": self.gc}
        self.dma_chans = [self.dS, self.dG]
        self.xd = [self.mkdma("xd0"), self.mkdma("xd1")]

    def sb(self, scope, name, shape, dt=F32):
        return scope.enter_context(self.nc.sbuf_tensor(name, shape, dt))

    def mkdma(self, name):
        c = Chan(self.nc, self.ctx, name)
        self.dma_chans.append(c)
        return c

    def borow(self):
        return self.wmr_t[0:1, 0:1024]

    def ps(self, scope, name, shape, dt=F32):
        return scope.enter_context(self.nc.psum_tensor(name, shape, dt))

    def dbg_out(self, name, shape, dt=F32):
        if name in self.dbg:
            t = self.nc.dram_tensor("dbg_" + name, shape, dt,
                                    kind="ExternalOutput").ap()
            self.dbg_t[name] = t
            return t
        return None

    def dbg_dump(self, name, src_ap, shape, dt=F32):
        t = self.dbg_out(name, shape, dt)
        if t is None:
            return
        self.barrier()

        @self.block.sync
        def _(sync):
            self.dS.inc(sync.dma_start(
                out=t[tuple(slice(None) for _ in shape)], in_=src_ap), 16)
            sync.wait_ge(self.dS.sem, self.dS.n)

    def barrier(self):
        marks = {}
        for e in ENGS:
            def body(eng, e=e):
                if e == "sync":
                    for c in self.dma_chans:
                        if c.n:
                            eng.wait_ge(c.sem, c.n)
                if e == "gpsimd":
                    eng.wait_ge(self.cc.sem, self.cc.n)
                marks[e] = self.chans[e].inc(enop(eng))
            getattr(self.block, e)(body)
        for e in ENGS:
            def body(eng, e=e):
                for o in ENGS:
                    if o != e:
                        eng.wait_ge(self.chans[o].sem, marks[o])
            getattr(self.block, e)(body)

    def declare_io(self):
        nc = self.nc
        dr = lambda name, shape, dt=F32: nc.dram_tensor(
            name, shape, dt, kind="ExternalInput").ap()
        self.i_xfull = dr("xfull", [S, D])
        self.i_xmine = dr("xmine", [TOWN, D])
        self.i_wq = dr("wq", [D, 512])
        self.i_wk = dr("wk", [D, 512])
        self.i_wv = dr("wv", [D, 512])
        self.i_bq = dr("bq", [1, 512])
        self.i_bk = dr("bk", [1, 512])
        self.i_bv = dr("bv", [1, 512])
        self.i_wo = dr("wo", [D, D])
        self.i_bo = dr("bo", [1, D])
        self.i_wu = dr("wu", [I3, D])
        self.i_wg = dr("wg", [I3, D])
        self.i_wd = dr("wd", [D, I3])
        self.i_crow = dr("crow", [1, D])
        self.i_wmod = dr("wmod", [D, 6 * D])
        self.i_identb = dr("identb", [128, 128], BF16)
        self.i_identf = dr("identf", [128, 128])
        self.i_maskb = dr("maskb", [128, 128], BF16)
        self.i_onesf = dr("onesf", [1, 128])
        self.i_onesb = dr("onesb", [1, 128], BF16)
        self.i_hselb = dr("hselb", [128, 2], BF16)
        self.i_halfidx = dr("halfidx", [1, 2], I32)
        self.i_eself = dr("eself", [64, 1024])

        self.o_out = nc.dram_tensor("out", [NT_OWN, 128, D], F32,
                                    kind="ExternalOutput").ap()

        dri = lambda name, shape, dt=F32: nc.dram_tensor(name, shape, dt,
                                                         kind="Internal")
        self.h_gate = dri("gate_raw_h", [ICH, 128, TOWN]).ap()
        self.h_up = dri("up_raw_h", [ICH, 128, TOWN]).ap()
        self.h_o = dri("o_raw_h", [ICH, 128, TOWN]).ap()
        self.h_wu = dri("wu_deq_h", [DCH, ICH, 128, 128], BF16).ap()
        self.h_wg = dri("wg_deq_h", [DCH, ICH, 128, 128], BF16).ap()
        self.h_wd = dri("wd_deq_h", [ICH, DCH, 128, 128], BF16).ap()
        self.h_xnew = dri("xnew_h", [NT_OWN, 128, D]).ap()
        self.g_ag1_in = dri("ag1_in", [TOWN, 512], BF16)
        self.g_ag2_in = dri("ag2_in", [TOWN, 512], BF16)
        self.g_ag_out = dri("ag_out", [4 * TOWN, 512], BF16)
        self.g_ar_in = [dri(f"ar_in{i}", [1, 256]) for i in range(4)]
        self.h_mrow = dri("mrow_h", [48, 128]).ap()
        self.g_ar_out = [dri(f"ar_out{i}", [1, 256]) for i in range(4)]

    def alloc_perm(self, sc):
        self.identb = self.sb(sc, "identb_s", [128, 128], BF16)
        self.identf = self.sb(sc, "identf_s", [128, 128])
        self.maskb = self.sb(sc, "maskb_s", [128, 128], BF16)
        self.onesf = self.sb(sc, "onesf_s", [1, 128])
        self.onesb = self.sb(sc, "onesb_s", [1, 128], BF16)
        self.hselb = self.sb(sc, "hselb_s", [128, 2], BF16)
        self.mod_fm = self.sb(sc, "mod_fm", [128, 128])
        self.modb = self.sb(sc, "modb", [128, 4, D])
        self.gabo = self.sb(sc, "gabo", [128, D])
        self.bqkv = self.sb(sc, "bqkv", [128, 3, MCH])
        self.bvb = self.sb(sc, "bvb", [128, 512])
        self.scales = self.sb(sc, "scales", [128, 16])
        self.expb = self.sb(sc, "expb", [128, 4])
        self.small = self.sb(sc, "small", [128, 64])
        self.row = self.sb(sc, "row", [1, 1024])
        self.silub = self.sb(sc, "silub", [128, 8], BF16)
        self.hofs = self.sb(sc, "hofs", [1, 2], I32)

    def p0(self, sc):
        nc, block = self.nc, self.block
        wmr = self.sb(sc, "wmr", [128, 3 * D // 2])
        self.wmr_t = wmr
        psA, psB = self.psA, self.psB
        psF = self.psC[:, 0:128]

        @block.sync
        def _(sync):
            for dst, src in ((self.identb, self.i_identb),
                             (self.identf, self.i_identf),
                             (self.maskb, self.i_maskb),
                             (self.onesf, self.i_onesf),
                             (self.onesb, self.i_onesb),
                             (self.hselb, self.i_hselb),
                             (self.hofs, self.i_halfidx)):
                self.dS.inc(sync.dma_start(out=dst[:, :], in_=src[:, :]), 16)
            self.dS.inc(sync.dma_start(
                out=self.small[:, 0:8],
                in_=self.i_crow.rearrange("o (c p) -> p (o c)", p=128),
                allow_slow_non_contiguous=True), 16)
            self.dS.inc(sync.dma_start(
                out=self.small[:, 8:12],
                in_=self.i_bq.rearrange("o (c p) -> p (o c)", p=128),
                allow_slow_non_contiguous=True), 16)
            self.dS.inc(sync.dma_start(
                out=self.small[:, 12:16],
                in_=self.i_bk.rearrange("o (c p) -> p (o c)", p=128),
                allow_slow_non_contiguous=True), 16)
            self.dS.inc(sync.dma_start(
                out=self.small[:, 16:20],
                in_=self.i_bv.rearrange("o (c p) -> p (o c)", p=128),
                allow_slow_non_contiguous=True), 16)
            self.dS.inc(sync.dma_start(out=self.row[0:1, 0:512],
                                       in_=self.i_bv[0:1, :]), 16)


        @block.gpsimd
        def _(g):
            for src, dst in ((self.i_wq, self.wq_s), (self.i_wk, self.wk_s),
                             (self.i_wv, self.wv_s)):
                sr = src.rearrange("(c p) m -> c p m", p=128)
                for c in range(DCH):
                    self.dG.inc(g.dma_start(out=dst[:, c, :], in_=sr[c, :, :]), 16)
            g.memset(self.v_tm[:, :, :].rearrange(
                "p t (h e) -> p t h e", e=65)[:, :, :, 64:65], 1.0)
            self.gc.inc(enop(g))
        self.barrier()

        @block.vector
        def _(v):
            v.memset(self.scales[:, :], 0.0)

        @block.scalar
        def _(s):
            s.activation(self.small[:, 32:40], self.small[:, 0:8], AF.Silu)
            self.sc.inc(enop(s))
        s_silu = self.sc.n

        @block.vector
        def _(v):
            self.sc.wait(v, s_silu)
            self.vc.inc(v.tensor_copy(self.silub[:, 0:8], self.small[:, 32:40]))
        v_silub = self.vc.n
        silu_f = self.small[:, 32:40]

        t0 = self.tc.n
        dg0 = self.dG.n

        @block.gpsimd
        def _(g):
            for ch in range(4 * DCH):
                c, qt = ch // 4, ch % 4
                if ch > 0:
                    g.wait_ge(self.tc.sem, t0 + ch)
                self.dG.inc(g.dma_start(
                    out=wmr[:, :],
                    in_=self.i_wmod[c * 128:(c + 1) * 128,
                                    qt * 1536:(qt + 1) * 1536]), 16)

        @block.tensor
        def _(t):
            self.vc.wait(t, v_silub)
            for ch in range(4 * DCH):
                c, qt = ch // 4, ch % 4
                t.wait_ge(self.dG.sem, dg0 + 16 * (ch + 1))
                for gg in range(12):
                    gcol = qt * 12 + gg
                    t.matmul(psA[:, gcol * 8 + c:gcol * 8 + c + 1],
                             wmr[:, gg * 128:(gg + 1) * 128],
                             silu_f[:, c:c + 1],
                             start=True, stop=True)
                self.tc.inc(enop(t))
        self.barrier()

        @block.vector
        def _(v):
            v.memset(self.mod_fm[:, 48:128], 0.0)
            v.tensor_reduce(self.mod_fm[:, 0:48],
                            psA[:, 0:384].rearrange("p (g c) -> p g c", c=8),
                            AX.X, ALU.add)
            v.drain()
            v.tensor_scalar_add(self.mod_fm[:, 8:16], self.mod_fm[:, 8:16], 1.0)
            v.tensor_scalar_add(self.mod_fm[:, 32:40], self.mod_fm[:, 32:40], 1.0)
            for lo in (16, 40):
                g = self.mod_fm[:, lo:lo + 8]
                v.drain()
                v.tensor_scalar(self.small[:, 40:48], g, -1.0, None, ALU.mult)
                v.drain()
                v.tensor_tensor(self.small[:, 40:48], g, self.small[:, 40:48],
                                ALU.max)
                v.drain()
                v.tensor_scalar_max(self.small[:, 40:48], self.small[:, 40:48],
                                    0.25)
                v.tensor_scalar(self.small[:, 48:56], g, 0.0, 2.0,
                                ALU.is_ge, ALU.mult)
                v.drain()
                v.tensor_scalar_add(self.small[:, 48:56], self.small[:, 48:56],
                                    -1.0)
                v.drain()
                v.tensor_tensor(g, self.small[:, 40:48], self.small[:, 48:56],
                                ALU.mult)
            v.drain()
            v.tensor_copy(self.silub[:, 0:8], self.mod_fm[:, 0:8])
        self.barrier()

        @block.tensor
        def _(t):
            for pj, w in enumerate((self.wq_s, self.wk_s, self.wv_s)):
                for mc in range(MCH):
                    j = pj * MCH + mc
                    for c in range(DCH):
                        t.matmul(psB[:, j:j + 1],
                                 w[:, c, mc * 128:(mc + 1) * 128],
                                 self.silub[:, c:c + 1],
                                 start=(c == 0), stop=(c == DCH - 1))
            for c in range(DCH):
                t.matmul(psA[0:1, 0:512], self.silub[:, c:c + 1],
                         self.wv_s[:, c, :],
                         start=(c == 0), stop=(c == DCH - 1))
            t.transpose(psF, self.mod_fm[:, :], self.identf[:, :])
        self.barrier()

        @block.vector
        def _(v):
            for pj in range(3):
                v.tensor_tensor(self.bqkv[:, pj, :],
                                self.small[:, 8 + 4 * pj:12 + 4 * pj],
                                psB[:, pj * MCH:(pj + 1) * MCH], ALU.add)
            v.tensor_tensor(self.row[0:1, 0:512], self.row[0:1, 0:512],
                            psA[0:1, 0:512], ALU.add)
        self.barrier()

        @block.vector
        def _(v):
            for w in (self.wq_s, self.wk_s, self.wv_s):
                for c in range(DCH):
                    i = v.tensor_scalar_mul(w[:, c, :], w[:, c, :],
                                            self.mod_fm[:, 8 + c:9 + c])
            self.vc.inc(i)
        v_wsc = self.vc.n

        @block.vector
        def _(v):
            v.tensor_copy(self.scr128[:, :], psF)
        self.barrier()

        @block.gpsimd
        def _(g):
            self.dG.inc(g.dma_start(out=self.h_mrow[:, :],
                                    in_=self.scr128[0:48, :]), 16)
        self.barrier()
        for gi, lo in enumerate((32, 24, 16, 40)):
            @block.gpsimd
            def _(g, gi=gi, lo=lo):
                self.dG.inc(g.dma_start(
                    out=self.rows4[0:1, :, :],
                    in_=self.h_mrow[lo:lo + 8, :].rearrange(
                        "a b -> (a b)").unsqueeze(0)), 16)
            self.barrier()
            t1 = self.tc.n
            v1 = self.vc.n

            @block.tensor
            def _(t, gi=gi):
                for k in range(8):
                    if k >= 2:
                        t.wait_ge(self.vc.sem, v1 + k - 1)
                    t.matmul(self.psD[:, 0:128] if k % 2 == 0
                             else self.psD[:, 128:256],
                             self.onesf[0:1, :], self.rows4[0:1, k, :],
                             start=True, stop=True)
                    self.tc.inc(enop(t))

            @block.vector
            def _(v, gi=gi):
                for k in range(8):
                    v.wait_ge(self.tc.sem, t1 + k + 1)
                    src = (self.psD[:, 0:128] if k % 2 == 0
                           else self.psD[:, 128:256])
                    self.vc.inc(v.tensor_copy(
                        self.modb[:, gi, k * 128:(k + 1) * 128], src))
            self.barrier()

        @block.sync
        def _(sync):
            self.dS.inc(sync.dma_start(out=self.borow()[0:1, 0:D],
                                       in_=self.i_bo[0:1, :]), 16)
        self.barrier()

        @block.tensor
        def _(t):
            t.matmul(psA[:, 0:512], self.onesf[0:1, :], self.row[0:1, 0:512],
                     start=True, stop=True)
            t.matmul(psB[:, 0:512], self.onesf[0:1, :],
                     self.borow()[0:1, 0:512], start=True, stop=True)
            t.matmul(self.psD[:, 0:512], self.onesf[0:1, :],
                     self.borow()[0:1, 512:1024], start=True, stop=True)
        self.barrier()

        @block.vector
        def _(v):
            v.memset(self.small[:, 63:64], LN_EPS)
            v.tensor_copy(self.bvb[:, :], psA[:, 0:512])
            v.tensor_copy(self.gabo[:, 0:512], psB[:, 0:512])
            v.tensor_copy(self.gabo[:, 512:1024], self.psD[:, 0:512])
            v.drain()
            v.tensor_tensor(self.gabo[:, :], self.gabo[:, :],
                            self.modb[:, 2, :], ALU.mult)
            v.tensor_copy(self.row[0:1, 200:202], self.hofs[0:1, :])
        self.barrier()

        @block.tensor
        def _(t):
            t.matmul(self.psD[:, 0:1], self.onesf[0:1, :],
                     self.row[0:1, 200:201], start=True, stop=True)
        self.barrier()

        @block.vector
        def _(v):
            v.tensor_copy(self.scales[:, 14:15], self.psD[:, 0:1])
        self.barrier()

    def p1(self):
        nc, block = self.nc, self.block
        hfm, xst, ht, psT = self.hfm, self.xst, self.ht, self.psT

        V0, T0, S0, D0 = self.vc.n, self.tc.n, self.sc.n, self.dS.n

        def v_stats(j):   return V0 + 32 * (j // 4) + 2 * (j % 4) + 1
        def v_apply(j):   return V0 + 32 * (j // 4) + 2 * (j % 4) + 2
        def v_hfm(j):     return V0 + 32 * (j // 4) + 8 + (j % 4) + 1
        def v_qkv(w, g):  return V0 + 32 * w + 12 + g + 1
        def v_nrm(w, i):  return V0 + 32 * w + 24 + i + 1
        def t_tr(j):      return T0 + 24 * (j // 4) + (j % 4) + 1
        def t_qkv(w, g):  return T0 + 24 * w + 4 + g + 1
        def t_nrm(w, i):  return T0 + 24 * w + 16 + i + 1
        def s_sqrt(j):    return S0 + 12 * (j // 4) + (j % 4) + 1
        def s_sq(w, i):   return S0 + 12 * w + 4 + i + 1
        @block.sync
        def _(sync):
            xf = self.i_xfull.rearrange("(t p) d -> t p d", p=128)
            for j in range(NT_ALL):
                if j >= 2:
                    sync.wait_ge(self.vc.sem, v_apply(j - 2))
                self.xd[j % 2].inc(sync.dma_start(out=xst[:, j % 2, :],
                                                  in_=xf[j, :, :]), 16)

        @block.vector
        def _(v):
            for w in range(8):
                for j4 in range(4):
                    j = 4 * w + j4
                    xt = xst[:, j % 2, :]
                    v.wait_ge(self.xd[j % 2].sem, 16 * (j // 2 + 1))
                    v.bn_stats(self.bn6[:, 0:6], xt[:, 0:512])
                    v.bn_stats(self.bn6[:, 6:12], xt[:, 512:1024])
                    v.drain()
                    v.bn_aggr(self.stats[:, 2 * j4:2 * j4 + 2], self.bn6[:, :])
                    self.vc.inc(enop(v))
                    assert self.vc.n == v_stats(j)
                    v.wait_ge(self.sc.sem, s_sqrt(j))
                    if j >= 2:
                        v.wait_ge(self.tc.sem, t_tr(j - 2))
                    v.reciprocal(self.stats[:, 8 + j4:9 + j4],
                                 self.stats[:, 12 + j4:13 + j4])
                    v.drain()
                    self.vc.inc(v.tensor_scalar(
                        ht[:, j % 2, :], xt,
                        self.stats[:, 2 * j4:2 * j4 + 1],
                        self.stats[:, 8 + j4:9 + j4],
                        ALU.subtract, ALU.mult))
                    assert self.vc.n == v_apply(j)
                for j4 in range(4):
                    j = 4 * w + j4
                    v.wait_ge(self.tc.sem, t_tr(j))
                    i = v.tensor_copy(
                        hfm[:, w % 2, :, j4 * 128:(j4 + 1) * 128],
                        psT[:, j % 2, :].rearrange("p (c t) -> p c t", c=DCH))
                    self.vc.inc(i)
                    assert self.vc.n == v_hfm(j)
                for g in range(12):
                    v.wait_ge(self.tc.sem, t_qkv(w, g))
                    pq = self.qkv_ps[g % 4]
                    if g < 4:
                        i = v.tensor_scalar_add(
                            self.q_fm[:, g, w * 512:(w + 1) * 512],
                            pq[:, :], self.bqkv[:, 0, g:g + 1])
                    elif g < 8:
                        i = v.tensor_scalar_add(
                            self.k_fm[:, g - 4, w * 512:(w + 1) * 512],
                            pq[:, :], self.bqkv[:, 1, g - 4:g - 3])
                    else:
                        j = 4 * w + (g - 8)
                        i = v.tensor_tensor(
                            self.v_tm[:, j, :].rearrange(
                                "p (h e) -> p h e", h=8)[:, :, 0:64],
                            pq[:, :].rearrange("p (h e) -> p h e", h=8),
                            self.bvb[:, :].rearrange("p (h e) -> p h e", h=8),
                            ALU.add)
                    self.vc.inc(i)
                    assert self.vc.n == v_qkv(w, g)
                for i in range(8):
                    v.wait_ge(self.tc.sem, t_nrm(w, i))
                    ii = v.tensor_reduce(self.qnst[0:2, i, w:w + 1],
                                         self.psN[0:2, :], AX.X, ALU.max)
                    self.vc.inc(ii)
                    assert self.vc.n == v_nrm(w, i)

        @block.scalar
        def _(s):
            for w in range(8):
                for j4 in range(4):
                    j = 4 * w + j4
                    s.wait_ge(self.vc.sem, v_stats(j))
                    s.activation(self.stats[:, 12 + j4:13 + j4],
                                 self.stats[:, 2 * j4 + 1:2 * j4 + 2], AF.Sqrt,
                                 bias=self.small[:, 63:64], scale=1.0)
                    self.sc.inc(enop(s))
                    assert self.sc.n == s_sqrt(j)
                for i in range(8):
                    qk, mc = i // 4, i % 4
                    s.wait_ge(self.vc.sem, v_qkv(w, 4 * qk + mc))
                    if not (w == 0 and i == 0):
                        pw, pi = (w, i - 1) if i > 0 else (w - 1, 7)
                        s.wait_ge(self.tc.sem, t_nrm(pw, pi))
                    src = self.q_fm if qk == 0 else self.k_fm
                    self.sc.inc(s.activation(
                        self.sqb[:, :],
                        src[:, mc, w * 512:(w + 1) * 512], AF.Square))
                    assert self.sc.n == s_sq(w, i)

        @block.tensor
        def _(t):
            for w in range(8):
                for j4 in range(4):
                    j = 4 * w + j4
                    t.wait_ge(self.vc.sem, v_apply(j))
                    if j >= 2:
                        t.wait_ge(self.vc.sem, v_hfm(j - 2))
                    for c in range(DCH):
                        i = t.transpose(psT[:, j % 2, c * 128:(c + 1) * 128],
                                        ht[:, j % 2, c * 128:(c + 1) * 128],
                                        self.identb[:, :])
                    self.tc.inc(i)
                    assert self.tc.n == t_tr(j)
                t.wait_ge(self.vc.sem, v_hfm(4 * w + 3))
                for g in range(12):
                    if w > 0 or g >= 4:
                        pw, pg = (w, g - 4) if g >= 4 else (w - 1, g + 8)
                        t.wait_ge(self.vc.sem, v_qkv(pw, pg))
                    pq = self.qkv_ps[g % 4]
                    if g < 8:
                        wgt = self.wq_s if g < 4 else self.wk_s
                        mc = g % 4
                        for c in range(DCH):
                            i = t.matmul(pq[:, :],
                                         wgt[:, c, mc * 128:(mc + 1) * 128],
                                         hfm[:, w % 2, c, :],
                                         start=(c == 0), stop=(c == DCH - 1))
                    else:
                        j4 = g - 8
                        for c in range(DCH):
                            i = t.matmul(pq[:, :],
                                         hfm[:, w % 2, c,
                                             j4 * 128:(j4 + 1) * 128],
                                         self.wv_s[:, c, :],
                                         start=(c == 0), stop=(c == DCH - 1))
                    self.tc.inc(i)
                    assert self.tc.n == t_qkv(w, g)
                for i in range(8):
                    t.wait_ge(self.sc.sem, s_sq(w, i))
                    if not (w == 0 and i == 0):
                        pw, pi = (w, i - 1) if i > 0 else (w - 1, 7)
                        t.wait_ge(self.vc.sem, v_nrm(pw, pi))
                    self.tc.inc(t.matmul(self.psN[:, :],
                                         self.hselb[:, :], self.sqb[:, :],
                                         start=True, stop=True))
                    assert self.tc.n == t_nrm(w, i)

        self.barrier()

    def p1b(self):
        block = self.block

        @block.vector
        def _(v):
            v.tensor_reduce(self.small[0:2, 0:8], self.qnst[:, :, :],
                            AX.X, ALU.max)
            v.drain()
            v.tensor_tensor(self.small[0:2, 8:12], self.small[0:2, 0:4],
                            self.small[0:2, 4:8], ALU.mult)
            self.vc.inc(enop(v))
        v_b2 = self.vc.n

        @block.gpsimd
        def _(g):
            g.wait_ge(self.vc.sem, v_b2)
            self.dG.inc(g.dma_start(
                out=self.row[0:1, 0:8:2], in_=self.small[0:1, 8:12],
                allow_slow_non_contiguous=True), 16)
            self.dG.inc(g.dma_start(
                out=self.row[0:1, 1:8:2], in_=self.small[1:2, 8:12],
                allow_slow_non_contiguous=True), 16)
        self.barrier()

        @block.scalar
        def _(s):
            s.activation(self.row[0:1, 8:16], self.row[0:1, 0:8], AF.Sqrt)
        self.barrier()

        @block.vector
        def _(v):
            v.tensor_tensor(self.row[0:1, 16:20], self.row[0:1, 8:16:2],
                            self.row[0:1, 9:16:2], ALU.max)
            v.drain()
            v.tensor_scalar_mul(self.row[0:1, 16:20], self.row[0:1, 16:20],
                                -0.125)
            v.drain()
            v.tensor_copy(self.silub[0:1, 0:4], self.row[0:1, 16:20])
        self.barrier()

        @block.tensor
        def _(t):
            t.matmul(self.psD[:, 0:4], self.onesb[0:1, :], self.silub[0:1, 0:4],
                     start=True, stop=True)
        self.barrier()

        @block.vector
        def _(v):
            v.tensor_copy(self.expb[:, :], self.psD[:, 0:4])
        self.barrier()


    def p2(self, NP, NCH):
        block = self.block
        S2 = (self.S2a, self.S2b)
        PVp = self.PVp
        ptx = (self.ptx0, self.ptx1)
        wld, bm, deqc, wTst = self.wld, self.bm, self.deqc, self.wTst

        groups = [(q4, hp, e) for q4 in range(4) for hp in range(4)
                  for e in range(2)]
        NG = len(groups)
        ngs = [8 * (q4 + 1) for (q4, hp, e) in groups]

        def pieces_of(off):
            return [(off, 512), (512, 1024)] if off < 512 else [(off, 1024)]

        def src_ap(k):
            if k < 24:
                return self.i_wu[k * 128:(k + 1) * 128, :]
            if k < 48:
                return self.i_wg[(k - 24) * 128:(k - 23) * 128, :]
            kk = k - 48
            r, q3 = kk // 3, kk % 3
            return self.i_wd[r * 128:(r + 1) * 128, q3 * 1024:(q3 + 1) * 1024]

        def dst_ap(k):
            if k < 24:
                return self.h_wu[:, k, :, :].rearrange("d p j -> p d j")
            if k < 48:
                return self.h_wg[:, k - 24, :, :].rearrange("d p j -> p d j")
            kk = k - 48
            r, q3 = kk // 3, kk % 3
            return self.h_wd[q3 * 8:(q3 + 1) * 8, r, :, :].rearrange(
                "i p j -> p i j")

        wl = [self.mkdma(f"wlb{i}") for i in range(4)]
        wt = [self.mkdma(f"wtb{i}") for i in range(2)]

        @block.vector
        def _(v):
            v.memset(S2[0][:, :], 0.0)
            v.memset(S2[1][:, :], 0.0)
            v.memset(PVp[:, :], 0.0)
        self.barrier()
        V0b, T0b, S0b = self.vc.n, self.tc.n, self.sc.n

        TSC, TPV, TTB, TTR = {}, {}, {}, {}
        SEX, VEV, VNM, VDQ, VET = {}, {}, {}, {}, {}
        tn = self.tc.n
        for g in range(NG):
            n_g = ngs[g]
            for c in range(n_g):
                tn += 1; TSC[(g, c)] = tn
                if c >= 1:
                    tn += 1; TPV[(g, c - 1)] = tn
            tn += 1; TPV[(g, n_g - 1)] = tn
            tn += 1; TTB[g] = tn
            p = g - 1
            if 0 <= p < NP:
                tn += 1; TTR[2 * p] = tn
                tn += 1; TTR[2 * p + 1] = tn
        for p in range(NG - 1, NP):
            tn += 1; TTR[2 * p] = tn
            tn += 1; TTR[2 * p + 1] = tn
        sn = self.sc.n
        for g in range(NG):
            for c in range(ngs[g]):
                sn += 1; SEX[(g, c)] = sn
        vn = self.vc.n
        for g in range(NG):
            vn += 1; VEV[g] = vn
            vn += 1; VNM[g] = vn
            p = g - 1
            if 0 <= p < NP:
                vn += 1; VET[2 * p] = vn
                vn += 1; VET[2 * p + 1] = vn
            if g < NP:
                vn += 1; VDQ[g] = vn
        for p in range(NG, NP + NG - 1):
            if p < NP:
                vn += 1; VDQ[p] = vn
            pe = p - 1
            if NG - 1 <= pe < NP:
                vn += 1; VET[2 * pe] = vn
                vn += 1; VET[2 * pe + 1] = vn

        def emit_deq_pair(v, p):
            ka, kb = 2 * p, 2 * p + 1
            v.wait_ge(wl[ka % 4].sem, 16 * (ka // 4 + 1))
            v.wait_ge(wl[kb % 4].sem, 16 * (kb // 4 + 1))
            wa, wb = wld[:, ka % 4, :], wld[:, kb % 4, :]
            wa3 = wa.rearrange("p (b j) -> p b j", j=32)
            wb3 = wb.rearrange("p (b j) -> p b j", j=32)
            ba, bb = bm[:, 0, :], bm[:, 1, :]
            v.tensor_reduce(ba[0:128, 0:32], wa3, AX.X, ALU.max,
                            apply_absolute_value=True)
            v.tensor_reduce(bb[0:128, 0:32], wb3, AX.X, ALU.max,
                            apply_absolute_value=True)
            v.drain()
            v.tensor_scalar(ba[0:128, 32:64], ba[0:128, 0:32], INV7, 1e-12,
                            ALU.mult, ALU.max)
            v.tensor_scalar(bb[0:128, 32:64], bb[0:128, 0:32], INV7, 1e-12,
                            ALU.mult, ALU.max)
            v.drain()
            v.reciprocal(ba[0:128, 64:96], ba[0:128, 32:64])
            v.reciprocal(bb[0:128, 64:96], bb[0:128, 32:64])
            v.drain()
            v.tensor_tensor(wa3, wa3, ba[0:128, 64:96].unsqueeze(2)
                            .broadcast_to((128, 32, 32)), ALU.mult)
            v.tensor_tensor(wb3, wb3, bb[0:128, 64:96].unsqueeze(2)
                            .broadcast_to((128, 32, 32)), ALU.mult)
            v.drain()
            v.tensor_scalar(wa, wa, RC, RC, ALU.add, ALU.subtract)
            v.tensor_scalar(wb, wb, RC, RC, ALU.add, ALU.subtract)
            v.drain()
            if p >= 1:
                v.wait_ge(self.tc.sem, TTR[2 * p - 1])
            v.tensor_tensor(deqc[:, 0, :].rearrange("p (b j) -> p b j", j=32),
                            wa3, ba[0:128, 32:64].unsqueeze(2)
                            .broadcast_to((128, 32, 32)), ALU.mult)
            self.vc.inc(v.tensor_tensor(
                deqc[:, 1, :].rearrange("p (b j) -> p b j", j=32),
                wb3, bb[0:128, 32:64].unsqueeze(2)
                .broadcast_to((128, 32, 32)), ALU.mult))
            assert self.vc.n == VDQ[p]

        def emit_evac_pair(v, p):
            for k in (2 * p, 2 * p + 1):
                v.wait_ge(self.tc.sem, TTR[k])
                if k >= 2:
                    v.wait_ge(wt[k % 2].sem, 16 * (k // 2))
                self.vc.inc(v.tensor_copy(
                    wTst[:, k % 2, :, :],
                    ptx[k % 2][:, :].rearrange("p (c j) -> p c j", j=128)))
                assert self.vc.n == VET[k]

        def emit_trans_pair(t, p):
            t.wait_ge(self.vc.sem, VDQ[p])
            if p >= 1:
                t.wait_ge(self.vc.sem, VET[2 * p - 1])
            for k in (2 * p, 2 * p + 1):
                for kb8 in range(8):
                    i = t.transpose(ptx[k % 2][:, kb8 * 128:(kb8 + 1) * 128],
                                    deqc[:, k % 2, kb8 * 128:(kb8 + 1) * 128],
                                    self.identb[:, :])
                self.tc.inc(i)
                assert self.tc.n == TTR[k]

        @block.tensor
        def _(t):
            for g, (q4, hp, e) in enumerate(groups):
                n_g = ngs[g]
                qb = 1024 * q4
                npp = ngs[g - 1] if g else 0
                for c in range(n_g):
                    if c >= 2:
                        t.wait_ge(self.sc.sem, SEX[(g, c - 2)])
                    elif g >= 1:
                        if c == 0:
                            t.wait_ge(self.sc.sem, SEX[(g - 1, npp - 2)])
                        else:
                            t.wait_ge(self.sc.sem, SEX[(g - 1, npp - 1)])
                            t.wait_ge(self.vc.sem, VNM[g - 1])
                    off = max(0, c * 128 - qb)
                    diag = c * 128 >= qb
                    for (a, b) in pieces_of(off):
                        i = t.matmul(
                            S2[c % 2][:, a:b],
                            self.k_fm[64 * e:64 * e + 64, hp,
                                      c * 128:(c + 1) * 128],
                            self.q_fm[64 * e:64 * e + 64, hp, qb + a:qb + b],
                            start=True, stop=not (diag and a <= off < b),
                            tile_position=(64 * e, 0))
                    if diag:
                        i = t.matmul(S2[c % 2][:, off:off + 128],
                                     self.identb[:, :], self.maskb[:, :],
                                     start=False, stop=True)
                    self.tc.inc(i)
                    assert self.tc.n == TSC[(g, c)]
                    if c >= 1:
                        cp = c - 1
                        t.wait_ge(self.sc.sem, SEX[(g, cp)])
                        if cp == 0 and g >= 1:
                            t.wait_ge(self.vc.sem, VEV[g - 1])
                        offp = max(0, cp * 128 - qb)
                        he = 2 * hp + e
                        for (a, b) in pieces_of(offp):
                            i = t.matmul(
                                PVp[0:65, a:b],
                                self.v_tm[:, cp, he * 65:he * 65 + 65],
                                self.p_buf[:, cp % 2,
                                           1024 * e + a:1024 * e + b],
                                start=(cp == 0), stop=(cp == n_g - 1),
                                skip_group_check=True)
                        self.tc.inc(i)
                        assert self.tc.n == TPV[(g, cp)]
                cp = n_g - 1
                t.wait_ge(self.sc.sem, SEX[(g, cp)])
                offp = max(0, cp * 128 - qb)
                he = 2 * hp + e
                for (a, b) in pieces_of(offp):
                    i = t.matmul(PVp[0:65, a:b],
                                 self.v_tm[:, cp, he * 65:he * 65 + 65],
                                 self.p_buf[:, cp % 2,
                                            1024 * e + a:1024 * e + b],
                                 start=(cp == 0), stop=True,
                                 skip_group_check=True)
                self.tc.inc(i)
                assert self.tc.n == TPV[(g, cp)]
                t.wait_ge(self.vc.sem, VEV[g])
                for k in range(8):
                    i = t.transpose(S2[1][:, k * 128:(k + 1) * 128],
                                    self.ostg[:, k * 128:(k + 1) * 128],
                                    self.identf[:, :])
                self.tc.inc(i)
                assert self.tc.n == TTB[g]
                p = g - 1
                if 0 <= p < NP:
                    emit_trans_pair(t, p)
            for p in range(NG - 1, NP):
                emit_trans_pair(t, p)

        @block.scalar
        def _(s):
            for g, (q4, hp, e) in enumerate(groups):
                n_g = ngs[g]
                qb = 1024 * q4
                for c in range(n_g):
                    s.wait_ge(self.tc.sem, TSC[(g, c)])
                    if c >= 2:
                        s.wait_ge(self.tc.sem, TPV[(g, c - 2)])
                    elif g >= 2:
                        n2 = ngs[g - 2]
                        s.wait_ge(self.tc.sem, TPV[(g - 2, n2 - 2 + c)])
                    off = max(0, c * 128 - qb)
                    self.sc.inc(s.activation(
                        self.p_buf[:, c % 2, 1024 * e + off:1024 * e + 1024],
                        S2[c % 2][:, off:1024], AF.Exp,
                        bias=self.expb[:, hp:hp + 1], scale=0.125))
                    assert self.sc.n == SEX[(g, c)]

        @block.vector
        def _(v):
            for g in range(NG):
                v.wait_ge(self.tc.sem, TPV[(g, ngs[g] - 1)])
                self.vc.inc(v.tensor_copy(self.ostg[:, :], PVp[:, :]))
                assert self.vc.n == VEV[g]
                v.wait_ge(self.tc.sem, TTB[g])
                v.reciprocal(self.rden[:, 0:8], S2[1][:, 64:1024:128])
                v.drain()
                q4, hp, e = groups[g]
                he = 2 * hp + e
                for k in range(8):
                    i = v.tensor_scalar_mul(
                        self.o_all[:, 8 * q4 + k, he * 64:(he + 1) * 64],
                        S2[1][:, k * 128:k * 128 + 64],
                        self.rden[:, k:k + 1])
                self.vc.inc(i)
                assert self.vc.n == VNM[g]
                p = g - 1
                if 0 <= p < NP:
                    emit_evac_pair(v, p)
                if g < NP:
                    emit_deq_pair(v, g)
            for p in range(NG, NP + NG - 1):
                if p < NP:
                    emit_deq_pair(v, p)
                pe = p - 1
                if NG - 1 <= pe < NP:
                    emit_evac_pair(v, pe)

        @block.gpsimd
        def _(g_):
            def ag(q4):
                gl = 8 * (q4 + 1) - 1
                agi = self.g_ag1_in if q4 == 1 else self.g_ag2_in
                tlo = 0 if q4 == 1 else 16
                g_.wait_ge(self.vc.sem, VNM[gl])
                self.dG.inc(g_.dma_start(
                    out=agi.ap().rearrange("(t p) h -> p t h", p=128),
                    in_=self.o_all[:, tlo:tlo + 16, :]), 16)
                g_.wait_ge(self.dG.sem, self.dG.n)
                g_.collective_compute(
                    "AllGather", ALU.bypass,
                    replica_groups=[[0, 1], [2, 3], [4, 5], [6, 7]],
                    ins=[agi.ap().opt()],
                    outs=[self.g_ag_out.ap()[(q4 - 1) * TOWN:
                                             (q4 + 1) * TOWN, :].opt()],
                ).then_inc(self.cc.sem, 1)
                self.cc.n += 1

            for p in range(NP):
                for k in (2 * p - 4, 2 * p - 3):
                    if k >= 0:
                        g_.wait_ge(self.vc.sem, VET[k])
                        wt[k % 2].inc(g_.dma_start(out=dst_ap(k),
                                                   in_=wTst[:, k % 2, :, :]),
                                      16)
                for k in (2 * p, 2 * p + 1):
                    if p >= 2:
                        g_.wait_ge(self.vc.sem, VDQ[p - 2])
                    wl[k % 4].inc(g_.dma_start(out=wld[:, k % 4, :],
                                               in_=src_ap(k)), 16)
                if p == 17:
                    ag(1)
            for k in range(2 * NP - 4, 2 * NP):
                g_.wait_ge(self.vc.sem, VET[k])
                wt[k % 2].inc(g_.dma_start(out=dst_ap(k),
                                           in_=wTst[:, k % 2, :, :]), 16)
            ag(3)
        self.barrier()

    def p3(self, sc):
        block = self.block
        olu = self.sb(sc, "olu", [128, 2, 2, D], BF16)
        osel = self.sb(sc, "osel", [128, 2, D], BF16)
        ofm = self.sb(sc, "ofm", [128, 2, DCH, 128], BF16)
        xt3 = self.sb(sc, "xt3", [128, 2, D])
        xnw = self.sb(sc, "xnw", [128, 2, D])
        ln2t = self.sb(sc, "ln2t", [128, D])
        h2r = self.sb(sc, "h2r", [128, NT_OWN, D])
        h2q = self.sb(sc, "h2q", [128, 2, D], BF16)
        st3 = self.sb(sc, "st3", [128, 16])
        bn3 = self.sb(sc, "bn3", [128, 12])
        scrP = self.sb(sc, "scrP", [128, 128])
        p3_ps = ExitStack()
        pA = self.ps(p3_ps, "pA3", [128, 512])
        pB = self.ps(p3_ps, "pB3", [128, 512])
        pT3 = self.ps(p3_ps, "pT3", [128, 1024], BF16)
        pSc = self.ps(p3_ps, "pSc", [128, 128])
        x3 = [self.mkdma("x3a"), self.mkdma("x3b")]
        xo3 = [self.mkdma("xo3a"), self.mkdma("xo3b")]

        @block.sync
        def _(sync):
            sync.wait_ge(self.cc.sem, self.cc.n)
        self.barrier()

        V0, T0, S0 = self.vc.n, self.tc.n, self.sc.n

        def t_tr3(t):  return T0 + 3 * t + 1
        def t_op(t, w2): return T0 + 3 * t + 2 + w2
        def v_sel(t):  return V0 + 7 * t + 1
        def v_ofm(t):  return V0 + 7 * t + 2
        def v_epi(t, w2): return V0 + 7 * t + 3 + w2
        def v_st2(t):  return V0 + 7 * t + 5
        def v_h2(t):   return V0 + 7 * t + 7
        def s_sq2(t):  return S0 + t + 1
        G0p3 = self.gc.n

        def g_xga(t):  return G0p3 + t + 1



        olc = [self.mkdma("olc0"), self.mkdma("olc1")]

        @block.sync
        def _(sync):
            ago = self.g_ag_out.ap().rearrange("(t p) h -> p t h", p=128)
            xm = self.i_xmine.rearrange("(t p) d -> t p d", p=128)
            for t in range(NT_OWN):
                if t >= 2:
                    sync.wait_ge(self.vc.sem, v_sel(t - 2))
                olc[t % 2].inc(sync.dma_start(out=olu[:, t % 2, 0, 0:512],
                                              in_=ago[:, t, :]), 16)
                olc[t % 2].inc(sync.dma_start(out=olu[:, t % 2, 0, 512:1024],
                                              in_=ago[:, 16 + t, :]), 16)
                olc[t % 2].inc(sync.dma_start(out=olu[:, t % 2, 1, 0:512],
                                              in_=ago[:, 32 + t, :]), 16)
                olc[t % 2].inc(sync.dma_start(out=olu[:, t % 2, 1, 512:1024],
                                              in_=ago[:, 48 + t, :]), 16)
                if t >= 2:
                    sync.wait_ge(self.vc.sem, v_epi(t - 2, 1))
                x3[t % 2].inc(sync.dma_start(out=xt3[:, t % 2, :],
                                             in_=xm[t, :, :]), 16)
                if t >= 1:
                    sync.wait_ge(self.vc.sem, v_epi(t - 1, 1))
                    xo3[(t - 1) % 2].inc(sync.dma_start(
                        out=self.h_xnew[t - 1, :, :],
                        in_=xnw[:, (t - 1) % 2, :]), 16)
            sync.wait_ge(self.vc.sem, v_epi(NT_OWN - 1, 1))
            xo3[(NT_OWN - 1) % 2].inc(sync.dma_start(
                out=self.h_xnew[NT_OWN - 1, :, :],
                in_=xnw[:, (NT_OWN - 1) % 2, :]), 16)

        @block.gpsimd
        def _(g):
            for t in range(NT_OWN):
                g.wait_ge(x3[t % 2].sem, 16 * (t // 2 + 1))
                if t >= 2:
                    g.wait_ge(self.vc.sem, v_epi(t - 2, 1))
                self.gc.inc(g.tensor_tensor(xt3[:, t % 2, :],
                                            xt3[:, t % 2, :],
                                            self.gabo[:, :], ALU.add))
                assert self.gc.n == g_xga(t)

        @block.tensor
        def _(t_):
            for t in range(NT_OWN):
                t_.wait_ge(self.vc.sem, v_sel(t))
                if t >= 2:
                    t_.wait_ge(self.vc.sem, v_ofm(t - 2))
                for c in range(DCH):
                    i = t_.transpose(pT3[:, c * 128:(c + 1) * 128],
                                     osel[:, t % 2, c * 128:(c + 1) * 128],
                                     self.identb[:, :])
                self.tc.inc(i)
                assert self.tc.n == t_tr3(t)
                t_.wait_ge(self.vc.sem, v_ofm(t))
                for w2 in range(2):
                    if t >= 1:
                        t_.wait_ge(self.vc.sem, v_epi(t - 1, w2))
                    pq = pA if w2 == 0 else pB
                    for c in range(DCH):
                        i = t_.matmul(pq[:, :], ofm[:, t % 2, c, :],
                                      self.wo_s[:, c, w2 * 512:(w2 + 1) * 512],
                                      start=(c == 0), stop=(c == DCH - 1))
                    self.tc.inc(i)
                    assert self.tc.n == t_op(t, w2)

        @block.vector
        def _(v):
            for t in range(NT_OWN):
                v.wait_ge(olc[t % 2].sem, 16 * 4 * (t // 2 + 1))
                if t >= 2:
                    v.wait_ge(self.tc.sem, t_tr3(t - 2))
                v.tensor_tensor(olu[:, t % 2, 1, :], olu[:, t % 2, 1, :],
                                olu[:, t % 2, 0, :], ALU.subtract)
                v.drain()
                self.vc.inc(v.scalar_tensor_tensor(
                    osel[:, t % 2, :], olu[:, t % 2, 1, :],
                    self.scales[:, 14:15], olu[:, t % 2, 0, :],
                    ALU.mult, ALU.add))
                assert self.vc.n == v_sel(t)
                v.wait_ge(self.tc.sem, t_tr3(t))
                self.vc.inc(v.tensor_copy(
                    ofm[:, t % 2, :, :],
                    pT3[:, :].rearrange("p (c j) -> p c j", j=128)))
                assert self.vc.n == v_ofm(t)
                for w2 in range(2):
                    v.wait_ge(self.tc.sem, t_op(t, w2))
                    if w2 == 0:
                        v.wait_ge(self.gc.sem, g_xga(t))
                        if t >= 2:
                            v.wait_ge(xo3[t % 2].sem, 16 * (t // 2))
                    pq = pA if w2 == 0 else pB
                    sl = slice(w2 * 512, (w2 + 1) * 512)
                    v.tensor_tensor(ln2t[:, sl], pq[:, :],
                                    self.modb[:, 2, sl], ALU.mult)
                    v.drain()
                    self.vc.inc(v.tensor_tensor(xnw[:, t % 2, sl],
                                                ln2t[:, sl],
                                                xt3[:, t % 2, sl], ALU.add))
                    assert self.vc.n == v_epi(t, w2)
                v.drain()
                v.bn_stats(bn3[:, 0:6], xnw[:, t % 2, 0:512])
                v.bn_stats(bn3[:, 6:12], xnw[:, t % 2, 512:1024])
                v.drain()
                v.bn_aggr(st3[:, 0:2], bn3[:, :])
                self.vc.inc(enop(v))
                assert self.vc.n == v_st2(t)
                v.wait_ge(self.sc.sem, s_sq2(t))
                v.reciprocal(st3[:, 2:3], st3[:, 3:4])
                v.drain()
                v.tensor_scalar(ln2t[:, 0:D], xnw[:, t % 2, :],
                                st3[:, 0:1], st3[:, 2:3],
                                ALU.subtract, ALU.mult)
                v.drain()
                self.vc.inc(v.tensor_tensor(ln2t[:, 0:D], ln2t[:, 0:D],
                                self.modb[:, 0, :], ALU.mult))
                v.drain()
                v.tensor_tensor(h2r[:, t, :], ln2t[:, 0:D],
                                self.modb[:, 1, :], ALU.add)
                v.drain()
                v.tensor_reduce(st3[:, 4:5], h2r[:, t, :], AX.X,
                                ALU.max, apply_absolute_value=True)
                v.drain()
                if t == 0:
                    i = v.tensor_copy(st3[:, 5:6], st3[:, 4:5])
                else:
                    i = v.tensor_tensor(st3[:, 5:6], st3[:, 5:6],
                                        st3[:, 4:5], ALU.max)
                self.vc.inc(i)
                assert self.vc.n == v_h2(t)

        @block.scalar
        def _(s_):
            for t in range(NT_OWN):
                s_.wait_ge(self.vc.sem, v_st2(t))
                s_.activation(st3[:, 3:4], st3[:, 1:2], AF.Sqrt,
                              bias=self.small[:, 63:64], scale=1.0)
                self.sc.inc(enop(s_))
                assert self.sc.n == s_sq2(t)

        self.barrier()
        self.allreduce_scale([st3[:, 5:6]], scrP, pSc, 0, [0])

        V1, T1 = self.vc.n, self.tc.n

        @block.vector
        def _(v):
            for t in range(NT_OWN):
                if t >= 2:
                    v.wait_ge(self.tc.sem, T1 + (t - 1))
                v.tensor_scalar(h2r[:, t, :], h2r[:, t, :],
                                self.scales[:, 1:2], RC, ALU.mult, ALU.add)
                v.drain()
                self.vc.inc(v.tensor_scalar(
                    h2q[:, t % 2, :], h2r[:, t, :], RC,
                    self.scales[:, 0:1], ALU.subtract, ALU.mult))
                if t >= 1:
                    v.wait_ge(self.tc.sem, T1 + t)
                    tt = t - 1
                    self.vc.inc(v.tensor_copy(
                        self.h2fm[:, :, tt * 128:(tt + 1) * 128],
                        pT3[:, :].rearrange("p (c j) -> p c j", j=128)))
            v.wait_ge(self.tc.sem, T1 + NT_OWN)
            tt = NT_OWN - 1
            self.vc.inc(v.tensor_copy(
                self.h2fm[:, :, tt * 128:(tt + 1) * 128],
                pT3[:, :].rearrange("p (c j) -> p c j", j=128)))

        @block.tensor
        def _(t_):
            for t in range(NT_OWN):
                t_.wait_ge(self.vc.sem, V1 + 2 * t + 1)
                if t >= 1:
                    t_.wait_ge(self.vc.sem, V1 + 2 * t)
                for c in range(DCH):
                    i = t_.transpose(pT3[:, c * 128:(c + 1) * 128],
                                     h2q[:, t % 2, c * 128:(c + 1) * 128],
                                     self.identb[:, :])
                self.tc.inc(i)
        self.barrier()
        p3_ps.close()

    def allreduce_scale(self, colmax_aps, scrP, pSc, buf_i, slots):
        block = self.block
        N = len(colmax_aps)

        @block.vector
        def _(v):
            v.memset(scrP[:, :], 0.0)
            v.drain()
            for i, ap in enumerate(colmax_aps):
                i2 = v.tensor_copy(scrP[:, 32 * i:32 * i + 1], ap)
            self.vc.inc(i2)
        v1 = self.vc.n

        @block.tensor
        def _(t):
            t.wait_ge(self.vc.sem, v1)
            self.tc.inc(t.transpose(pSc[:, :], scrP[:, :], self.identf[:, :]))
        t1 = self.tc.n

        @block.vector
        def _(v):
            v.wait_ge(self.tc.sem, t1)
            for i in range(N):
                i2 = v.tensor_copy(self.row[0:1, 128 * i:128 * (i + 1)],
                                   pSc[32 * i:32 * i + 1, :])
            self.vc.inc(i2)
        v2 = self.vc.n

        @block.gpsimd
        def _(g):
            g.wait_ge(self.vc.sem, v2)
            self.dG.inc(g.dma_start(out=self.g_ar_in[buf_i].ap()[0:1, 0:128 * N],
                                    in_=self.row[0:1, 0:128 * N]), 16)
            g.wait_ge(self.dG.sem, self.dG.n)
            g.collective_compute(
                "AllReduce", ALU.max, replica_groups=[list(range(8))],
                ins=[self.g_ar_in[buf_i].ap()[0:1, 0:128 * N].opt()],
                outs=[self.g_ar_out[buf_i].ap()[0:1, 0:128 * N].opt()],
            ).then_inc(self.cc.sem, 1)
            self.cc.n += 1
            g.wait_ge(self.cc.sem, self.cc.n)
            self.dG.inc(g.dma_start(out=self.row[0:1, 0:128 * N],
                                    in_=self.g_ar_out[buf_i].ap()[0:1, 0:128 * N]),
                        16)
        d1 = self.dG.n

        @block.vector
        def _(v):
            v.wait_ge(self.dG.sem, d1)
            for i in range(N):
                v.tensor_reduce(self.row[0:1, 512 + 2 * i:513 + 2 * i],
                                self.row[0:1, 128 * i:128 * (i + 1)],
                                AX.X, ALU.max)
            v.drain()
            for i in range(N):
                v.tensor_scalar(self.row[0:1, 512 + 2 * i:513 + 2 * i],
                                self.row[0:1, 512 + 2 * i:513 + 2 * i],
                                INV15, 1e-12, ALU.mult, ALU.max)
            v.drain()
            for i in range(N):
                i2 = v.reciprocal(self.row[0:1, 513 + 2 * i:514 + 2 * i],
                                  self.row[0:1, 512 + 2 * i:513 + 2 * i])
            self.vc.inc(i2)
        v3 = self.vc.n

        @block.tensor
        def _(t):
            t.wait_ge(self.vc.sem, v3)
            t.matmul(pSc[:, 0:2 * N], self.onesf[0:1, :],
                     self.row[0:1, 512:512 + 2 * N], start=True, stop=True)
            self.tc.inc(enop(t))
        t2 = self.tc.n

        @block.vector
        def _(v):
            v.wait_ge(self.tc.sem, t2)
            for i, slot in enumerate(slots):
                v.tensor_copy(self.scales[:, 2 * slot:2 * slot + 2],
                              pSc[:, 2 * i:2 * i + 2])
        self.barrier()

    def p4(self, sc):
        block = self.block
        wsg = self.sb(sc, "wsg", [128, 2, DCH, 128], BF16)
        wsu = self.sb(sc, "wsu", [128, 2, DCH, 128], BF16)
        gstg = self.sb(sc, "gstg", [128, 2, TOWN])
        ustg = self.sb(sc, "ustg", [128, 2, TOWN])
        mx4 = self.sb(sc, "mx4", [128, 200])
        scrP = self.sb(sc, "scrP4", [128, 128])
        p4_ps = ExitStack()
        pG = [self.ps(p4_ps, "pG0", [128, 512]), self.ps(p4_ps, "pG1", [128, 512])]
        pU = [self.ps(p4_ps, "pU0", [128, 512]), self.ps(p4_ps, "pU1", [128, 512])]
        pSc = self.ps(p4_ps, "pSc4", [128, 128])
        wg4 = [self.mkdma("wg4a"), self.mkdma("wg4b")]
        wu4 = [self.mkdma("wu4a"), self.mkdma("wu4b")]
        sp4 = [self.mkdma("sp4a"), self.mkdma("sp4b")]

        V0, T0, S0 = self.vc.n, self.tc.n, self.sc.n

        def t_mm(ic, win, gu): return T0 + 8 * ic + 2 * win + gu + 1
        def v_gev(ic, win):    return V0 + 8 * ic + 2 * win + 1
        def v_urd(ic, win):    return V0 + 8 * ic + 2 * win + 2
        def s_uev(ic, win):    return S0 + 4 * ic + win + 1

        @block.gpsimd
        def _(g):
            for ic in range(ICH):
                if ic >= 2:
                    g.wait_ge(self.tc.sem, t_mm(ic - 1, 3, 1))
                wg4[ic % 2].inc(g.dma_start(
                    out=wsg[:, ic % 2, :, :],
                    in_=self.h_wg[:, ic, :, :].rearrange("d p j -> p d j")), 16)
                wu4[ic % 2].inc(g.dma_start(
                    out=wsu[:, ic % 2, :, :],
                    in_=self.h_wu[:, ic, :, :].rearrange("d p j -> p d j")), 16)
                if ic >= 2:
                    icp = ic - 2
                    g.wait_ge(self.vc.sem, v_gev(icp, 3))
                    sp4[icp % 2].inc(g.dma_start(out=self.h_gate[icp, :, :],
                                                 in_=gstg[:, icp % 2, :]), 16)
                    g.wait_ge(self.vc.sem, v_urd(icp, 3))
                    sp4[icp % 2].inc(g.dma_start(out=self.h_up[icp, :, :],
                                                 in_=ustg[:, icp % 2, :]), 16)
            for icp in (ICH - 2, ICH - 1):
                g.wait_ge(self.vc.sem, v_gev(icp, 3))
                sp4[icp % 2].inc(g.dma_start(out=self.h_gate[icp, :, :],
                                             in_=gstg[:, icp % 2, :]), 16)
                g.wait_ge(self.vc.sem, v_urd(icp, 3))
                sp4[icp % 2].inc(g.dma_start(out=self.h_up[icp, :, :],
                                             in_=ustg[:, icp % 2, :]), 16)

        @block.tensor
        def _(t):
            for ic in range(ICH):
                t.wait_ge(wg4[ic % 2].sem, 16 * (ic // 2 + 1))
                t.wait_ge(wu4[ic % 2].sem, 16 * (ic // 2 + 1))
                for win in range(4):
                    sl = slice(win * 512, (win + 1) * 512)
                    pic, pwin = (ic, win - 2) if win >= 2 else (ic - 1, win + 2)
                    if pic >= 0:
                        t.wait_ge(self.vc.sem, v_gev(pic, pwin))
                    for c in range(DCH):
                        i = t.matmul(pG[win % 2][:, :], wsg[:, ic % 2, c, :],
                                     self.h2fm[:, c, sl],
                                     start=(c == 0), stop=(c == DCH - 1))
                    self.tc.inc(i)
                    assert self.tc.n == t_mm(ic, win, 0)
                    if pic >= 0:
                        t.wait_ge(self.sc.sem, s_uev(pic, pwin))
                        t.wait_ge(self.vc.sem, v_urd(pic, pwin))
                    for c in range(DCH):
                        i = t.matmul(pU[win % 2][:, :], wsu[:, ic % 2, c, :],
                                     self.h2fm[:, c, sl],
                                     start=(c == 0), stop=(c == DCH - 1))
                    self.tc.inc(i)
                    assert self.tc.n == t_mm(ic, win, 1)

        @block.vector
        def _(v):
            for ic in range(ICH):
                for win in range(4):
                    sl = slice(win * 512, (win + 1) * 512)
                    v.wait_ge(self.tc.sem, t_mm(ic, win, 0))
                    if ic >= 2 and win == 0:
                        v.wait_ge(sp4[ic % 2].sem, 32 * (ic // 2))
                    kc = 4 * ic + win
                    v.tensor_copy(gstg[:, ic % 2, sl], pG[win % 2][:, :])
                    self.vc.inc(v.tensor_reduce(mx4[:, kc:kc + 1],
                                                pG[win % 2][:, :], AX.X,
                                                ALU.max,
                                                apply_absolute_value=True))
                    assert self.vc.n == v_gev(ic, win)
                    v.wait_ge(self.sc.sem, s_uev(ic, win))
                    self.vc.inc(v.tensor_reduce(mx4[:, 100 + kc:101 + kc],
                                                pU[win % 2][:, :], AX.X,
                                                ALU.max,
                                                apply_absolute_value=True))
                    assert self.vc.n == v_urd(ic, win)

        @block.scalar
        def _(s):
            for ic in range(ICH):
                for win in range(4):
                    sl = slice(win * 512, (win + 1) * 512)
                    s.wait_ge(self.tc.sem, t_mm(ic, win, 1))
                    if ic >= 2 and win == 0:
                        s.wait_ge(sp4[ic % 2].sem, 32 * (ic // 2))
                    s.copy(ustg[:, ic % 2, sl], pU[win % 2][:, :])
                    self.sc.inc(enop(s))
                    assert self.sc.n == s_uev(ic, win)

        @block.vector
        def _(v):
            v.drain()
            v.tensor_reduce(mx4[:, 196:197], mx4[:, 0:96], AX.X, ALU.max)
            v.tensor_reduce(mx4[:, 197:198], mx4[:, 100:196], AX.X, ALU.max)
        self.barrier()
        self.allreduce_scale([mx4[:, 196:197], mx4[:, 197:198]], scrP, pSc,
                             1, [1, 2])
        p4_ps.close()

    def p56(self, sc):
        block = self.block
        self.qg = self.sb(sc, "qg", [128, ICH, TOWN], I16)
        grd = self.sb(sc, "grd", [128, 3, TOWN])
        sgf = self.sb(sc, "sgf", [128, 2, TOWN])
        tf = self.sb(sc, "tf", [128, TOWN])
        osp = self.sb(sc, "osp", [128, 2, TOWN])
        mx5 = self.sb(sc, "mx5", [128, ICH + 2])
        scrP = self.sb(sc, "scrP5", [128, 128])
        p5_ps = ExitStack()
        pSc = self.ps(p5_ps, "pSc5", [128, 128])
        g5 = [self.mkdma("g5a"), self.mkdma("g5b"), self.mkdma("g5c")]
        o5 = [self.mkdma("o5a"), self.mkdma("o5b")]

        sG, rG = self.scales[:, 2:3], self.scales[:, 3:4]
        sU, rU = self.scales[:, 4:5], self.scales[:, 5:6]
        s3, r3p = self.scales[:, 6:7], self.scales[:, 10:11]

        S0, V0 = self.sc.n, self.vc.n
        sA, sB, vA, vB = {}, {}, {}, {}
        cnt = S0
        for cyc in range(ICH + 1):
            if cyc < ICH:
                cnt += 1; sA[cyc] = cnt
            if cyc >= 1:
                cnt += 1; sB[cyc - 1] = cnt
        cnt = V0
        for cyc in range(ICH + 2):
            if cyc < ICH:
                cnt += 1; vA[cyc] = cnt
            if cyc >= 2:
                cnt += 1; vB[cyc - 2] = cnt

        @block.sync
        def _(sync):
            for k in range(ICH):
                if k >= 3:
                    sync.wait_ge(self.vc.sem, vA[k - 3])
                g5[k % 3].inc(sync.dma_start(out=grd[:, k % 3, :],
                                             in_=self.h_gate[k, :, :]), 16)

        @block.scalar
        def _(s):
            for cyc in range(ICH + 1):
                if cyc < ICH:
                    s.wait_ge(g5[cyc % 3].sem, 16 * (cyc // 3 + 1))
                    self.sc.inc(s.activation(grd[:, cyc % 3, :],
                                             grd[:, cyc % 3, :], AF.Copy,
                                             bias=RC, scale=rG))
                    assert self.sc.n == sA[cyc]
                if cyc >= 1:
                    k = cyc - 1
                    s.wait_ge(self.vc.sem, vB[k - 2] if k >= 2 else vA[k])
                    s.activation(sgf[:, k % 2, :], self.qg[:, k, :],
                                 AF.Sigmoid, scale=sG)
                    s.drain()
                    self.sc.inc(s.activation(sgf[:, k % 2, :],
                                             sgf[:, k % 2, :], AF.Copy,
                                             bias=RC, scale=65536.0))
                    assert self.sc.n == sB[k]

        @block.vector
        def _(v):
            for cyc in range(ICH + 2):
                if cyc < ICH:
                    v.wait_ge(self.sc.sem, sA[cyc])
                    self.vc.inc(v.tensor_scalar(self.qg[:, cyc, :],
                                                grd[:, cyc % 3, :],
                                                RC, None, ALU.subtract))
                    assert self.vc.n == vA[cyc]
                if cyc >= 2:
                    k = cyc - 2
                    v.wait_ge(self.sc.sem, sB[k])
                    v.tensor_scalar(sgf[:, k % 2, :], sgf[:, k % 2, :],
                                    RC, 65535.0, ALU.subtract, ALU.min)
                    v.drain()
                    v.scalar_tensor_tensor(
                        tf[:, :], self.qg[:, k, :], sG,
                        sgf[:, k % 2, :], ALU.mult, ALU.mult)
                    v.drain()
                    self.vc.inc(v.tensor_reduce(
                        mx5[:, k:k + 1], tf[:, :], AX.X, ALU.max,
                        apply_absolute_value=True))
                    assert self.vc.n == vB[k]
            v.drain()
            v.tensor_reduce(mx5[:, ICH:ICH + 1], mx5[:, 0:ICH], AX.X, ALU.max)
            v.drain()
            v.tensor_scalar_mul(mx5[:, ICH:ICH + 1], mx5[:, ICH:ICH + 1],
                                1.0 / 65536.0)

        self.barrier()
        self.allreduce_scale([mx5[:, ICH:ICH + 1]], scrP, pSc, 2, [3])

        @block.vector
        def _(v):
            v.tensor_scalar(self.scales[:, 10:11], self.scales[:, 7:8],
                            1.0 / 65536.0, None, ALU.mult)
        self.barrier()

        S1, V1 = self.sc.n, self.vc.n
        sA1, sA2, sB2, vA2, vB2 = {}, {}, {}, {}, {}
        cnt = S1
        for cyc in range(ICH + 2):
            if cyc >= 2:
                cnt += 1; sB2[cyc - 2] = cnt
            if cyc < ICH:
                cnt += 1; sA1[cyc] = cnt
                cnt += 1; sA2[cyc] = cnt
        cnt = V1
        for cyc in range(ICH + 2):
            if cyc >= 2:
                cnt += 1; vB2[cyc - 2] = cnt
            if cyc >= 1 and cyc <= ICH:
                cnt += 1; vA2[cyc - 1] = cnt

        @block.sync
        def _(sync):
            for k in range(ICH):
                if k >= 3:
                    sync.wait_ge(self.vc.sem, vB2[k - 3])
                    o5[(k - 3) % 2].inc(sync.dma_start(
                        out=self.h_o[k - 3, :, :],
                        in_=osp[:, (k - 3) % 2, :]), 16)
                g5[k % 3].inc(sync.dma_start(out=grd[:, k % 3, :],
                                             in_=self.h_up[k, :, :]), 16)
            for k in range(ICH - 3, ICH):
                sync.wait_ge(self.vc.sem, vB2[k])
                o5[k % 2].inc(sync.dma_start(out=self.h_o[k, :, :],
                                             in_=osp[:, k % 2, :]), 16)

        @block.scalar
        def _(s):
            for cyc in range(ICH + 2):
                if cyc >= 2:
                    k = cyc - 2
                    s.wait_ge(self.vc.sem, vA2[k])
                    self.sc.inc(s.activation(tf[:, :], tf[:, :],
                                             AF.Copy, bias=RC, scale=r3p))
                    assert self.sc.n == sB2[k]
                if cyc < ICH:
                    s.wait_ge(g5[cyc % 3].sem,
                              16 * (ICH // 3 + cyc // 3 + 1))
                    self.sc.inc(s.activation(grd[:, cyc % 3, :],
                                             grd[:, cyc % 3, :], AF.Copy,
                                             bias=RC, scale=rU))
                    assert self.sc.n == sA1[cyc]
                    s.activation(sgf[:, cyc % 2, :], self.qg[:, cyc, :],
                                 AF.Sigmoid, scale=sG)
                    s.drain()
                    self.sc.inc(s.activation(sgf[:, cyc % 2, :],
                                             sgf[:, cyc % 2, :], AF.Copy,
                                             bias=RC, scale=65536.0))
                    assert self.sc.n == sA2[cyc]

        @block.vector
        def _(v):
            for cyc in range(ICH + 2):
                if cyc >= 2:
                    k = cyc - 2
                    v.wait_ge(self.sc.sem, sB2[k])
                    v.tensor_scalar(self.qg[:, k, :], tf[:, :],
                                    RC, None, ALU.subtract)
                    v.drain()
                    if k >= 2:
                        v.wait_ge(o5[k % 2].sem, 16 * (k // 2))
                    v.scalar_tensor_tensor(
                        osp[:, k % 2, :], self.qg[:, k, :], s3,
                        grd[:, k % 3, :], ALU.mult, ALU.mult)
                    v.drain()
                    self.vc.inc(v.tensor_reduce(
                        mx5[:, k:k + 1], osp[:, k % 2, :], AX.X, ALU.max,
                        apply_absolute_value=True))
                    assert self.vc.n == vB2[k]
                if cyc >= 1 and cyc <= ICH:
                    k1 = cyc - 1
                    v.wait_ge(self.sc.sem, sA1[k1])
                    v.tensor_scalar(grd[:, k1 % 3, :], grd[:, k1 % 3, :],
                                    RC, sU, ALU.subtract, ALU.mult)
                    v.wait_ge(self.sc.sem, sA2[k1])
                    v.tensor_scalar(sgf[:, k1 % 2, :], sgf[:, k1 % 2, :],
                                    RC, 65535.0, ALU.subtract, ALU.min)
                    v.drain()
                    self.vc.inc(v.scalar_tensor_tensor(
                        tf[:, :], self.qg[:, k1, :], sG,
                        sgf[:, k1 % 2, :], ALU.mult, ALU.mult))
                    assert self.vc.n == vA2[k1]
            v.drain()
            v.tensor_reduce(mx5[:, ICH + 1:ICH + 2], mx5[:, 0:ICH], AX.X,
                            ALU.max)

        self.barrier()
        self.allreduce_scale([mx5[:, ICH + 1:ICH + 2]], scrP, pSc, 3, [4])
        p5_ps.close()

    def p7(self, sc):
        block = self.block
        OBR = ICH + 2
        obf = self.sb(sc, "obf", [128, OBR, 512], BF16)
        oin = self.sb(sc, "oin", [128, 4, 512])
        ofq = self.sb(sc, "ofq", [128, 2, 512])
        wdst = self.sb(sc, "wdst", [128, ICH, DCH, 128], BF16)
        xnw3 = self.sb(sc, "xnw3", [128, 4, D])
        res3 = self.sb(sc, "res3", [128, 2, D])
        p7_ps = ExitStack()
        pD = [self.ps(p7_ps, f"pD{j}", [128, 512]) for j in range(8)]
        oi = [self.mkdma(f"oi{i}") for i in range(4)]
        w7 = [self.mkdma(f"w7{i}") for i in range(4)]
        x7 = [self.mkdma(f"x7{i}") for i in range(4)]
        o7 = [self.mkdma("o7a"), self.mkdma("o7b")]

        sO, rO = self.scales[:, 8:9], self.scales[:, 9:10]

        V0, T0, S0 = self.vc.n, self.tc.n, self.sc.n
        voq, vepi, sq1 = {}, {}, {}
        cnt = V0
        for w in range(4):
            for ic in range(ICH):
                cnt += 1
                voq[(w, ic)] = cnt
            if w >= 1:
                for j4 in range(4):
                    cnt += 1
                    vepi[(w - 1, j4)] = cnt
        for j4 in range(4):
            cnt += 1
            vepi[(3, j4)] = cnt
        cnt = S0
        for w in range(4):
            for ic in range(ICH):
                cnt += 1
                sq1[(w, ic)] = cnt

        def t7(w, ic):  return T0 + ICH * w + ic + 1

        @block.gpsimd
        def _(g):
            for ic in range(ICH):
                if ic >= 4:
                    g.wait_ge(self.tc.sem, t7(0, ic - 4))
                w7[ic % 4].inc(g.dma_start(
                    out=wdst[:, ic, :, :],
                    in_=self.h_wd[ic, :, :, :].rearrange(
                        "d p j -> p d j")), 16)

        @block.sync
        def _(sync):
            for w in range(4):
                for ic in range(ICH):
                    gi = w * ICH + ic
                    if gi >= 4:
                        sync.wait_ge(self.sc.sem,
                                     sq1[divmod(gi - 4, ICH)])
                    oi[gi % 4].inc(sync.dma_start(
                        out=oin[:, gi % 4, :],
                        in_=self.h_o[ic, :, w * 512:(w + 1) * 512]), 16)
                for j4 in range(4):
                    t = w * 4 + j4
                    if t >= 4:
                        sync.wait_ge(self.vc.sem, vepi[((t - 4) // 4,
                                                        (t - 4) % 4)])
                    x7[t % 4].inc(sync.dma_start(
                        out=xnw3[:, t % 4, :],
                        in_=self.h_xnew[t, :, :]), 16)

        @block.scalar
        def _(s):
            for w in range(4):
                for ic in range(ICH):
                    gi = w * ICH + ic
                    s.wait_ge(oi[gi % 4].sem, 16 * (gi // 4 + 1))
                    if gi >= 2:
                        s.wait_ge(self.vc.sem, voq[divmod(gi - 2, ICH)])
                    self.sc.inc(s.activation(ofq[:, gi % 2, :],
                                             oin[:, gi % 4, :], AF.Copy,
                                             bias=RC, scale=rO))
                    assert self.sc.n == sq1[(w, ic)]
                if w >= 1:
                    for j4 in range(4):
                        t = (w - 1) * 4 + j4
                        s.wait_ge(self.vc.sem, vepi[(w - 1, j4)])
                        o7[t % 2].inc(s.dma_start(out=self.o_out[t, :, :],
                                                  in_=res3[:, t % 2, :]), 16)
            for j4 in range(4):
                t = 12 + j4
                s.wait_ge(self.vc.sem, vepi[(3, j4)])
                o7[t % 2].inc(s.dma_start(out=self.o_out[t, :, :],
                                          in_=res3[:, t % 2, :]), 16)

        @block.vector
        def _(v):
            def emit_oq(w, ic):
                gi = w * ICH + ic
                v.wait_ge(self.sc.sem, sq1[(w, ic)])
                if gi >= OBR:
                    pw, pic = divmod(gi - OBR, ICH)
                    v.wait_ge(self.tc.sem, t7(pw, pic))
                self.vc.inc(v.tensor_scalar(obf[:, gi % OBR, :],
                                            ofq[:, gi % 2, :], RC, sO,
                                            ALU.subtract, ALU.mult))
                assert self.vc.n == voq[(w, ic)]

            def emit_epi(w):
                for j4 in range(4):
                    t = w * 4 + j4
                    v.wait_ge(self.tc.sem, t7(w, ICH - 1))
                    v.wait_ge(x7[t % 4].sem, 16 * (t // 4 + 1))
                    if t >= 2:
                        v.wait_ge(o7[t % 2].sem, 16 * (t // 2))
                    for dw2 in range(2):
                        dsl = slice(dw2 * 512, (dw2 + 1) * 512)
                        v.tensor_tensor(res3[:, t % 2, dsl],
                                        pD[4 * dw2 + j4][:, :],
                                        self.modb[:, 3, dsl], ALU.mult)
                        v.drain()
                        v.tensor_tensor(res3[:, t % 2, dsl],
                                        res3[:, t % 2, dsl],
                                        xnw3[:, t % 4, dsl], ALU.add)
                    self.vc.inc(enop(v))
                    assert self.vc.n == vepi[(w, j4)]

            for w in range(4):
                for ic in range(ICH):
                    emit_oq(w, ic)
                if w >= 1:
                    emit_epi(w - 1)
            emit_epi(3)

        @block.tensor
        def _(t_):
            for w in range(4):
                for ic in range(ICH):
                    t_.wait_ge(self.vc.sem, voq[(w, ic)])
                    t_.wait_ge(w7[ic % 4].sem, 16 * (ic // 4 + 1))
                    if ic == 0 and w >= 1:
                        t_.wait_ge(self.vc.sem, vepi[(w - 1, 3)])
                    gi = w * ICH + ic
                    for j4 in range(4):
                        for dw2 in range(2):
                            t_.matmul(pD[4 * dw2 + j4][:, :],
                                      obf[:, gi % OBR, j4 * 128:(j4 + 1) * 128],
                                      wdst[:, ic, dw2 * 4:dw2 * 4 + 4, :]
                                      .rearrange("p a b -> p (a b)"),
                                      start=(ic == 0), stop=(ic == ICH - 1))
                    self.tc.inc(enop(t_))
                    assert self.tc.n == t7(w, ic)

        self.barrier()
        p7_ps.close()

    def build(self):
        self.declare_io()
        ctx = self.ctx
        perm_sc = ctx.enter_context(ExitStack())
        self.alloc_perm(perm_sc)

        attn_sc = ExitStack()
        self.q_fm = self.sb(attn_sc, "q_fm", [128, MCH, S], BF16)
        self.k_fm = self.sb(attn_sc, "k_fm", [128, MCH, S], BF16)
        self.v_tm = self.sb(attn_sc, "v_tm", [128, NT_ALL, 8 * 65], BF16)

        qkv_sc = ExitStack()
        self.wq_s = self.sb(qkv_sc, "wq_s", [128, DCH, 512], BF16)
        self.wk_s = self.sb(qkv_sc, "wk_s", [128, DCH, 512], BF16)
        self.wv_s = self.sb(qkv_sc, "wv_s", [128, DCH, 512], BF16)
        self.hfm = self.sb(qkv_sc, "hfm", [128, 2, DCH, 512], BF16)
        self.xst = self.sb(qkv_sc, "xst", [128, 2, D])
        self.ht = self.sb(qkv_sc, "ht", [128, 2, D], BF16)
        self.sqb = self.sb(qkv_sc, "sqb", [128, 512], BF16)
        self.stats = self.sb(qkv_sc, "stats", [128, 16])
        self.bn6 = self.sb(qkv_sc, "bn6", [128, 12])
        self.rows4 = self.sb(qkv_sc, "rows4", [1, 8, 128])
        self.scr128 = self.sb(qkv_sc, "scr128", [128, 128])
        self.qnst = self.sb(qkv_sc, "qnst", [2, 8, 8])

        qkv_ps_sc = ExitStack()
        self.psA = self.ps(qkv_ps_sc, "psA", [128, 512])
        self.psB = self.ps(qkv_ps_sc, "psB", [128, 512])
        self.psC = self.ps(qkv_ps_sc, "psC", [128, 512])
        self.psD = self.ps(qkv_ps_sc, "psD", [128, 512])
        self.psN = self.ps(qkv_ps_sc, "psN", [2, 512])
        self.psT = self.ps(qkv_ps_sc, "psT", [128, 2, 1024], BF16)
        self.qkv_ps = [self.psA, self.psB, self.psC, self.psD]

        with ExitStack() as p0_sc:
            self.p0(p0_sc)
        self.p1()
        self.p1b()

        self.dbg_dump("qfm", self.q_fm[:, :, :], [128, MCH, S], BF16)
        self.dbg_dump("kfm", self.k_fm[:, :, :], [128, MCH, S], BF16)
        self.dbg_dump("vtm", self.v_tm[:, :, :], [128, NT_ALL, 8 * 65], BF16)
        self.dbg_dump("modfm", self.mod_fm[:, 0:48], [128, 48])
        self.dbg_dump("modb", self.modb[:, :, :], [128, 4, D])
        self.dbg_dump("gabo", self.gabo[:, :], [128, D])
        self.dbg_dump("expb", self.expb[:, :], [128, 4])

        qkv_sc.close()
        qkv_ps_sc.close()

        self.o_all = self.sb(attn_sc, "o_all", [128, NT_ALL, 512], BF16)
        self.ostg = self.sb(attn_sc, "ostg", [128, 1024])
        self.rden = self.sb(attn_sc, "rden", [128, 8])
        self.p_buf = self.sb(attn_sc, "p_buf", [128, 2, 2048], BF16)
        p2m_sc = ExitStack()
        self.wld = self.sb(p2m_sc, "wld", [128, 4, 1024])
        self.bm = self.sb(p2m_sc, "bm", [128, 2, 96])
        self.deqc = self.sb(p2m_sc, "deqc", [128, 2, 1024], BF16)
        self.wTst = self.sb(p2m_sc, "wTst", [128, 2, 8, 128], BF16)
        p2_ps = ExitStack()
        self.S2a = self.ps(p2_ps, "S2a", [128, 1024])
        self.S2b = self.ps(p2_ps, "S2b", [128, 1024])
        self.PVp = self.ps(p2_ps, "PVp", [128, 1024])
        self.ptx0 = self.ps(p2_ps, "ptx0", [128, 1024], BF16)
        self.ptx1 = self.ps(p2_ps, "ptx1", [128, 1024], BF16)
        self.p2(NP=36, NCH=72)
        self.dbg_dump("oall", self.o_all[:, :, :], [128, NT_ALL, 512], BF16)
        p2_ps.close()
        p2m_sc.close()
        attn_sc.close()

        p34_sc = ExitStack()
        self.h2fm = self.sb(p34_sc, "h2fm", [128, DCH, TOWN], BF16)
        self.wo_s = self.sb(p34_sc, "wo_s", [128, DCH, D], BF16)

        @self.block.gpsimd
        def _(g):
            wor = self.i_wo.rearrange("(c p) m -> c p m", p=128)
            for c in range(DCH):
                self.dG.inc(g.dma_start(out=self.wo_s[:, c, :],
                                        in_=wor[c, :, :]), 16)
        p3_sc = ExitStack()
        self.p3(p3_sc)
        self.dbg_dump("h2fm", self.h2fm[:, :, :], [128, DCH, TOWN], BF16)
        self.dbg_dump("scl", self.scales[:, :], [128, 16])
        p3_sc.close()

        p4_sc = ExitStack()
        self.p4(p4_sc)
        p4_sc.close()
        p34_sc.close()

        p56_sc = ExitStack()
        self.p56(p56_sc)
        self.dbg_dump("scl2", self.scales[:, :], [128, 16])
        self.dbg_dump("qgd", self.qg[:, :, :], [128, ICH, TOWN], I16)
        p56_sc.close()

        p7_sc = ExitStack()
        self.p7(p7_sc)
        p7_sc.close()

        self.barrier()



def _consts():
    identb = np.eye(128, dtype=ml_dtypes.bfloat16)
    identf = np.eye(128, dtype=np.float32)
    m = np.where(np.arange(128)[None, :] >= np.arange(128)[:, None],
                 np.float32(0.0), np.float32(-1e30))
    maskb = m.astype(ml_dtypes.bfloat16)
    onesf = np.ones((1, 128), np.float32)
    onesb = np.ones((1, 128), ml_dtypes.bfloat16)
    hselb = np.zeros((128, 2), ml_dtypes.bfloat16)
    hselb[0:64, 0] = 1
    hselb[64:128, 1] = 1
    eself = np.zeros((64, 1024), np.float32)
    for kk in range(64):
        for b in range(32):
            if kk % 32 == b:
                eself[kk, 32 * b:32 * b + 32] = 1.0
    return identb, identf, maskb, onesf, onesb, hselb, eself


def stage_inputs(inputs):
    x = np.asarray(inputs["L0_attn_x"], np.float32)
    wq = np.asarray(inputs["L0_attn_wq"], np.float32)
    wk = np.asarray(inputs["L0_attn_wk"], np.float32)
    wv = np.asarray(inputs["L0_attn_wv"], np.float32)
    bq = np.asarray(inputs["L0_attn_bq"], np.float32).reshape(1, D)
    bk = np.asarray(inputs["L0_attn_bk"], np.float32).reshape(1, D)
    bv = np.asarray(inputs["L0_attn_bv"], np.float32).reshape(1, D)
    wo = np.asarray(inputs["L0_attn_wo"], np.float32)
    bo = np.asarray(inputs["L0_attn_bo"], np.float32).reshape(1, D)
    wu = np.asarray(inputs["L0_ffn_w_up"], np.float32)
    wg = np.asarray(inputs["L0_ffn_w_gate"], np.float32)
    wd = np.asarray(inputs["L0_ffn_w_down"], np.float32)
    cc = np.asarray(inputs["L0_c"], np.float32)
    wmod = np.asarray(inputs["L0_wmod"], np.float32)
    identb, identf, maskb, onesf, onesb, hselb, eself = _consts()

    in_maps = []
    for core in range(8):
        b, half = core // 2, core % 2
        hsl = slice(half * 512, half * 512 + 512)
        in_maps.append({
            "xfull": np.ascontiguousarray(x[b]),
            "xmine": np.ascontiguousarray(x[b, half * TOWN:(half + 1) * TOWN]),
            "wq": np.ascontiguousarray(wq[:, hsl]),
            "wk": np.ascontiguousarray(wk[:, hsl]),
            "wv": np.ascontiguousarray(wv[:, hsl]),
            "bq": np.ascontiguousarray(bq[:, hsl]),
            "bk": np.ascontiguousarray(bk[:, hsl]),
            "bv": np.ascontiguousarray(bv[:, hsl]),
            "wo": wo, "bo": bo, "wu": wu, "wg": wg, "wd": wd,
            "crow": np.ascontiguousarray(cc[b:b + 1]),
            "wmod": wmod,
            "identb": identb, "identf": identf, "maskb": maskb,
            "onesf": onesf, "onesb": onesb, "hselb": hselb, "eself": eself,
            "halfidx": np.array([[half, 1 - half]], np.int32),
        })
    return in_maps


def build(dbg=()):
    nc = bass.Bass("TRN2", target_bir_lowering=False, debug=False)
    with ExitStack() as ctx:
        with nc.Block() as block:
            kb = KB(nc, ctx, block, dbg)
            kb.build()
    return nc, kb


def kernel(**inputs):
    nc, kb = build()
    in_maps = stage_inputs(inputs)
    res = run_bass_kernel_spmd(nc, in_maps, core_ids=list(range(8)))
    out = np.zeros((B, S, D), np.float32)
    for core in range(8):
        b, half = core // 2, core % 2
        out[b, half * TOWN:(half + 1) * TOWN] = \
            res.results[core]["out"].reshape(TOWN, D)
    return out
